# version 19
# baseline (speedup 1.0000x reference)
import contextlib
import numpy as np
import ml_dtypes
import concourse.bass as bass
import concourse.mybir as mybir
from concourse.bass_utils import run_bass_kernel_spmd

F32 = mybir.dt.float32
BF16 = mybir.dt.bfloat16
AF = mybir.ActivationFunctionType
OP = mybir.AluOpType
AX = mybir.AxisListType

B = 2
L = 2048
D = 1024
DI = 2048
NST = 16
RK = 64
FF = 12288
ITER = 4096
TC = 256
SC = 512
NTC = L // TC
NSC = L // SC
TT = 512
G = 8
NCC = D // 128
NDC = DI // 128
NFC = FF // 128
EPS = 1e-5
OVL_TH = float(np.float64(65536.0) - np.float64(np.float32(0.7)) * 65536.0)
BIS_ITERS = 36
NEG_BIG = -3.0e38

STAGES = ("mamba", "ffn", "final")
PARANOID = False
DEBUG_SPILL = False
DEBUG_MAMBA = False
DEBUG_FFN = False
DEBUG_FFN_ONLY = False
DEBUG_ITERS = (1, 2, 3)


class Buf:
    __slots__ = ("w", "rs")
    def __init__(self):
        self.w = None
        self.rs = []


class Op:
    __slots__ = ("sec", "stream", "emit", "deps", "inc", "idx", "tag")
    def __init__(self, sec, stream, emit, deps, idx, tag):
        self.sec = sec; self.stream = stream; self.emit = emit
        self.deps = deps; self.idx = idx; self.tag = tag; self.inc = 0


class Sched:
    NSUB = {"qsp": 4, "qgp": 8, "qact": 4, "qix": 16}
    SECTIONS = ("pe", "act", "dve", "pool", "sp")
    STREAMS = tuple(["pe", "act", "dve", "pool", "qcc"]
                    + [f"qsp{i}" for i in range(4)] + [f"qgp{i}" for i in range(8)]
                    + [f"qact{i}" for i in range(4)] + [f"qix{i}" for i in range(16)])

    def __init__(self, nc):
        self.nc = nc
        self.ops = []
        self.final_deps = []
        self._pending_fence = None
        self._qctr = {}
        self._qlast = {}

    def add(self, sec, stream, emit, deps=(), tag=""):
        deps = [d for d in deps if d is not None]
        if stream in self.NSUB:
            i = self._qctr.get(stream, 0)
            self._qctr[stream] = i + 1
            sub = f"{stream}{i % self.NSUB[stream]}"
            prev = self._qlast.get(sub)
            if prev is not None:
                deps = deps + [prev]
            stream = sub
        if self._pending_fence is not None and stream in self._pending_fence:
            deps = deps + self._pending_fence.pop(stream)
        op = Op(sec, stream, emit, sorted(set(deps)), len(self.ops), tag)
        self.ops.append(op)
        if stream.startswith("q") and stream != "qcc":
            self._qlast[stream] = op.idx
        return op.idx

    def u(self, sec, stream, emit, r=(), w=(), deps=(), tag=""):
        d = list(deps)
        for b in r:
            if b.w is not None:
                d.append(b.w)
        for b in w:
            if b.w is not None:
                d.append(b.w)
            d.extend(b.rs)
        t = self.add(sec, stream, emit, d, tag)
        for b in r:
            b.rs.append(t)
        for b in w:
            b.w = t
            b.rs = []
        return t

    def fence(self):
        last = {}
        for op in self.ops:
            last[op.stream] = op.idx
        toks = list(last.values())
        self._pending_fence = {st: list(toks) for st in self.STREAMS}

    def emit_all(self):
        nc = self.nc
        tgt = set()
        for op in self.ops:
            tgt.update(op.deps)
        if PARANOID:
            tgt.update(range(len(self.ops)))
        tgt.update(self.final_deps)
        last_of = {}
        for op in self.ops:
            last_of[op.stream] = op.idx
        tgt.update(last_of.values())
        val = {st: 0 for st in self.STREAMS}
        opval = [0] * len(self.ops)
        for op in self.ops:
            if op.idx in tgt or op.stream.startswith("q"):
                op.inc = 16 if (op.stream.startswith("q") and op.stream != "qcc") else 1
                val[op.stream] += op.inc
            opval[op.idx] = val[op.stream]
        self._opval = opval

        eng_of = {"pe": nc.tensor, "act": nc.scalar, "dve": nc.vector,
                  "pool": nc.gpsimd, "sp": nc.sync}
        with contextlib.ExitStack() as st:
            sems = {x: st.enter_context(nc.semaphore("sem_" + x))
                    for x in self.STREAMS}
            block = st.enter_context(nc.Block())

            def emit_section(sec):
                eng = eng_of[sec]
                known = {x: 0 for x in self.STREAMS}
                for op in self.ops:
                    if op.sec != sec:
                        continue
                    for d in op.deps:
                        dop = self.ops[d]
                        if (not PARANOID) and dop.sec == sec and not dop.stream.startswith("q"):
                            continue
                        v = self._opval[d]
                        if known[dop.stream] >= v:
                            continue
                        eng.wait_ge(sems[dop.stream], v)
                        known[dop.stream] = v
                    try:
                        inst = op.emit() if op.emit is not None else None
                    except Exception as e:
                        raise RuntimeError(f"emit failed op={op.idx} tag={op.tag}: {e}") from e
                    if op.inc:
                        if inst is None:
                            raise RuntimeError(f"op {op.idx} {op.tag}: dep target lacks instruction")
                        inst.then_inc(sems[op.stream], op.inc)
                if sec == "pool":
                    for d in self.final_deps:
                        dop = self.ops[d]
                        v = self._opval[d]
                        if known[dop.stream] < v:
                            eng.wait_ge(sems[dop.stream], v)
                            known[dop.stream] = v

            @block.tensor
            def _(eng):
                emit_section("pe")

            @block.scalar
            def _(eng):
                emit_section("act")

            @block.vector
            def _(eng):
                emit_section("dve")

            @block.gpsimd
            def _(eng):
                emit_section("pool")

            @block.sync
            def _(eng):
                emit_section("sp")


C_INB = 0
C_DTB = 16
C_CONVW = 32
C_CONVB = 80
C_D = 96
C_OUTB = 112
C_XPB = 120
C_XPBM = 121
C_K = 122
C_ALOG = 126
NCB = C_ALOG + NDC * NST
K_VALS = (EPS, 1.0, -1.0, -0.5)


def build(nc, stages=STAGES):
    s = Sched(nc)
    T = {}

    def dram(name, shape, dt, kind="ExternalInput"):
        T[name] = nc.dram_tensor(name, list(shape), dt, kind=kind).ap()
        return T[name]

    x_in = dram("x", [B, NCC, 128, L], F32)
    out = dram("out", [B, NCC, 128, L], F32, kind="ExternalOutput")
    wi = dram("wi", [D, DI], F32)
    wo = dram("wo", [DI, D], F32)
    wxp = dram("wxp", [DI, RK + NST], F32)
    wdt = dram("wdt", [RK, DI], F32)
    wm = dram("wm", [D, FF], F32)
    wmR = dram("wmR", [FF, D], F32)
    fidx_c = dram("fidx_c", [128, 96], F32)
    idx_sp = {(k, b): dram(f"idx{k}_{b}_sp", [ITER, 1], F32, kind="Internal")
              for k in (2, 3) for b in range(B)}
    wp = dram("wp", [FF, D], F32)
    wc1 = dram("wc1", [D, 256], F32)
    wc2 = dram("wc2", [256, FF], F32)
    cb = dram("cb", [128, NCB], F32)
    ones_c = dram("ones_c", [128, 128], F32)
    onescol_c = dram("onescol_c", [128, 1], F32)
    onescolb_c = dram("onescolb_c", [128, 1], BF16)
    ident_c = dram("ident_c", [128, 128], F32)
    ltri_c = dram("ltri_c", [128, 128], F32)
    bhist_c = dram("bhist_c", [128, 96], F32)
    oneh_c = dram("oneh_c", [16, NST * 128], F32)
    dbk = "ExternalOutput" if DEBUG_SPILL else "Internal"
    wi_bf = dram("wi_bf", [D, DI], BF16, kind="Internal")
    wo_bf = dram("wo_bf", [DI, D], BF16, kind="Internal")
    x1_sp = dram("x1_sp", [B, NCC, 128, L], F32, kind=dbk)
    xin1_sp = dram("xin1_sp", [B, NCC, 128, L], BF16, kind="ExternalInput" if DEBUG_FFN_ONLY else dbk)
    xn_sp = {k: dram(f"xn{k}_sp", [B, NCC, 128, L], BF16, kind=dbk)
             for k in (1, 2, 3)}
    xin3_sp = dram("xin3_sp", [B, NCC, 128, L], BF16, kind=dbk)
    if DEBUG_MAMBA:
        dbg = {
            "dbg_u": dram("dbg_u", [128, NDC * SC], BF16, kind="ExternalOutput"),
            "dbg_dlr": dram("dbg_dlr", [64, SC], BF16, kind="ExternalOutput"),
            "dbg_bm": dram("dbg_bm", [16, SC], BF16, kind="ExternalOutput"),
            "dbg_bmbc": dram("dbg_bmbc", [128, NST * SC], BF16, kind="ExternalOutput"),
            "dbg_delta": dram("dbg_delta", [128, SC], F32, kind="ExternalOutput"),
            "dbg_w": dram("dbg_w", [128, SC], F32, kind="ExternalOutput"),
            "dbg_dA": dram("dbg_dA", [128, SC], F32, kind="ExternalOutput"),
            "dbg_dBu": dram("dbg_dBu", [128, SC], BF16, kind="ExternalOutput"),
            "dbg_S": dram("dbg_S", [128, 8 * SC], BF16, kind="ExternalOutput"),
            "dbg_y": dram("dbg_y", [128, NDC * SC], BF16, kind="ExternalOutput"),
            "dbg_attn": dram("dbg_attn", [128, NCC * SC], BF16, kind="ExternalOutput"),
        }
    if DEBUG_FFN:
        dram("dbg_h", [128, G * TT], BF16, kind="ExternalOutput")
        dram("dbg_xi", [128, NCC * TT], BF16, kind="ExternalOutput")
        dram("dbg_acc", [128, NCC * L], BF16, kind="ExternalOutput")
    cc_in = {k: dram(f"cc_in{k}", [1, 8], F32, kind="Internal") for k in (1, 2)}
    cc_out = {k: dram(f"cc_out{k}", [1, 8], F32, kind="Internal") for k in (1, 2)}

    pb = [nc.alloc_psum_tensor(f"pb{i}", [128, 512], F32).ap() for i in range(8)]

    bufs = {}
    def bf(name):
        if name not in bufs:
            bufs[name] = Buf()
        return bufs[name]
    pbuf = [bf(f"pb{i}") for i in range(8)]

    def sb(name, shape, dt, stack=None):
        if stack is None:
            return nc.alloc_sbuf_tensor(name, list(shape), dt).ap()
        return stack.enter_context(nc.sbuf_tensor(name, list(shape), dt)).ap()

    cb_sb = sb("cb_sb", [128, NCB], F32)
    A_sb = sb("A_sb", [128, NDC * NST], F32)
    ones_sb = sb("ones_sb", [128, 1], F32)
    ones_bf = sb("ones_bf", [128, 1], BF16)
    ones1_sb = sb("ones1_sb", [1, 128], F32)
    ident_sb = sb("ident_sb", [128, 128], F32)
    ltri_sb = sb("ltri_sb", [128, 128], F32)
    bhist_sb = sb("bhist_sb", [128, 96], F32)
    oneh_sb = sb("oneh_sb", [16, NST * 128], BF16)
    ln_sc = sb("ln_sc", [1, 4 * 512], F32)
    sqf_sb = sb("sqf_sb", [128, 512], F32)
    sqb_sb = sb("sqb_sb", [128, 512], BF16)
    fin_sc = sb("fin_sc", [128, 512], F32)
    scal_sb = sb("scal_sb", [1, 16], F32)
    mask_sb = {(k, b): sb(f"mask{k}_{b}", [128, 96], F32)
               for k in (1, 2, 3) for b in range(B)}
    hist2_sb = [sb(f"hist2_{b}", [128, 96], F32) for b in range(B)]
    d_bc = {nm: sb("dbc_" + nm, [128, 1], F32) for nm in ("d2", "d3")}
    pooled_sb = sb("pooled_sb", [128, NCC * 2], F32)

    for i, v in enumerate(K_VALS):
        nc.const_aps.register(F32, float(v), cb_sb[:, C_K + i: C_K + i + 1])
    s.u("sp", "qsp", lambda: nc.sync.dma_start(out=cb_sb, in_=cb), w=[bf("cb")])
    s.u("sp", "qsp", lambda: nc.sync.dma_start(out=ones_sb, in_=onescol_c), w=[bf("ones")])
    s.u("sp", "qsp", lambda: nc.sync.dma_start(out=ones1_sb, in_=ones_c[0:1, :]), w=[bf("ones1")])
    s.u("sp", "qsp", lambda: nc.sync.dma_start(out=ident_sb, in_=ident_c), w=[bf("ident")])
    s.u("sp", "qsp", lambda: nc.sync.dma_start(out=ltri_sb, in_=ltri_c), w=[bf("ltri")])
    s.u("sp", "qsp", lambda: nc.sync.dma_start(out=bhist_sb, in_=bhist_c), w=[bf("bhist")])
    s.u("pool", "qgp", lambda: nc.gpsimd.dma_start(out=oneh_sb, in_=oneh_c), w=[bf("oneh")])
    s.u("sp", "qsp", lambda: nc.sync.dma_start(out=ones_bf, in_=onescolb_c), w=[bf("onesbf")])
    s.u("act", "act", lambda: nc.scalar.activation(A_sb, cb_sb[:, C_ALOG:C_ALOG + NDC * NST], AF.Exp),
        r=[bf("cb")], w=[bf("A")], tag="Aexp")
    s.u("dve", "dve", lambda: nc.vector.tensor_scalar_mul(A_sb, A_sb, -1.0), w=[bf("A")], tag="Aneg")
    s.u("dve", "dve", lambda: nc.vector.memset(scal_sb, 0.0), w=[bf("scal")])

    def cbcol(off, idx, n=1):
        return cb_sb[:, off + idx: off + idx + n]

    ln_ctr = [0]

    def ln(srcs, Tn, tag):
        fp32src = srcs[0][0].tensor.dtype == F32
        one = ones_sb if fp32src else ones_bf
        onebuf = bf("ones") if fp32src else bf("onesbf")
        sq = sqf_sb if fp32src else sqb_sb
        sqB = bf("sqf" if fp32src else "sqb")
        ps_sum = pb[6][:1, :Tn]
        ps_sq = pb[7][:1, :Tn]
        nsrc = len(srcs)
        for i, (ap, ab) in enumerate(srcs):
            s.u("dve", "dve", lambda ap=ap: nc.vector.tensor_mul(sq[:, :Tn], ap, ap),
                r=[ab], w=[sqB], tag=tag + "sq")
            s.u("pe", "pe", lambda ap=ap, i=i: nc.tensor.matmul(
                ps_sum, one, ap, start=(i == 0), stop=(i == nsrc - 1)),
                r=[ab, onebuf], w=[pbuf[6]], tag=tag + "msum")
            s.u("pe", "pe", lambda i=i: nc.tensor.matmul(
                ps_sq, one, sq[:, :Tn], start=(i == 0), stop=(i == nsrc - 1)),
                r=[sqB, onebuf], w=[pbuf[7]], tag=tag + "msq")
        kk = ln_ctr[0] % 2
        ln_ctr[0] += 1
        mrow = ln_sc[:, kk * 1024: kk * 1024 + Tn]
        vrow = ln_sc[:, kk * 1024 + 512: kk * 1024 + 512 + Tn]
        rowb = bf(f"lnrow{kk}")
        s.u("dve", "dve", lambda: nc.vector.tensor_scalar_mul(mrow, ps_sum, 1.0 / D),
            r=[pbuf[6]], w=[rowb], tag=tag + "mean")
        s.u("dve", "dve", lambda: nc.vector.tensor_mul(vrow, mrow, mrow),
            r=[rowb], w=[rowb], tag=tag + "m2")
        s.u("dve", "dve", lambda: nc.vector.scalar_tensor_tensor(
            vrow, ps_sq, 1.0 / D, vrow, OP.mult, OP.subtract),
            r=[pbuf[7], rowb], w=[rowb], tag=tag + "var")
        s.u("act", "act", lambda: nc.scalar.activation(vrow, vrow, AF.Ln, bias=EPS),
            r=[rowb, bf("cb")], w=[rowb], tag=tag + "lnv")
        s.u("act", "act", lambda: nc.scalar.activation(vrow, vrow, AF.Exp, scale=-0.5),
            r=[rowb, bf("cb")], w=[rowb], tag=tag + "rstd")
        mbc = pb[6][:, :Tn]
        rbc = pb[7][:, :Tn]
        s.u("pe", "pe", lambda: nc.tensor.matmul(mbc, ones1_sb, mrow, start=True, stop=True),
            r=[rowb, bf("ones1")], w=[pbuf[6]], tag=tag + "mbc")
        s.u("pe", "pe", lambda: nc.tensor.matmul(rbc, ones1_sb, vrow, start=True, stop=True),
            r=[rowb, bf("ones1")], w=[pbuf[7]], tag=tag + "rbc")
        return mbc, rbc, pbuf[6], pbuf[7]

    ph1 = contextlib.ExitStack()
    xld = [sb(f"xld{i}", [128, NCC * TC], BF16, ph1) for i in range(2)]
    ep_x = sb("ep_x", [128, NCC * SC], F32, ph1)
    hpad = [sb(f"hpad{i}", [128, NDC * (TC + 2)], BF16, ph1) for i in range(2)]
    lcol2 = [sb(f"lcol2_{b}", [128, NDC * 2], BF16, ph1) for b in range(B)]
    wi_rb = [sb(f"wi_rb{i}", [128, 8 * 256], BF16, ph1) for i in range(2)]
    wo_rb = [sb(f"wo_rb{i}", [128, NDC * 128], BF16, ph1) for i in range(2)]
    wxp_sb = sb("wxp_sb", [128, NDC * (RK + NST)], BF16, ph1)
    wdt_sb = sb("wdt_sb", [64, DI], BF16, ph1)
    u_sc = [sb(f"u_sc{i}", [128, NDC * SC], BF16, ph1) for i in range(2)]
    dlr_bf = [sb(f"dlr_bf{i}", [64, SC], BF16, ph1) for i in range(2)]
    bm_sb = [sb(f"bm_sb{i}", [16, SC], BF16, ph1) for i in range(2)]
    bmbc = sb("bmbc", [128, NST * SC], BF16, ph1)
    delta_t = [sb(f"delta{i}", [128, SC], F32, ph1) for i in range(3)]
    w_t = [sb(f"w_t{i}", [128, SC], F32, ph1) for i in range(3)]
    cv_t = [sb(f"cv_t{i}", [128, TC], F32, ph1) for i in range(2)]
    dA_t = [sb(f"dA{i}", [128, SC], F32, ph1) for i in range(3)]
    dBu_t = [sb(f"dBu{i}", [128, SC], BF16, ph1) for i in range(3)]
    S_h = sb("S_h", [128, 8 * SC], BF16, ph1)
    rd1 = sb("rd1", [128, SC], F32, ph1)
    rd2 = sb("rd2", [128, SC], F32, ph1)
    yred2 = sb("yred2", [128, SC], F32, ph1)
    ytr = sb("ytr", [128, 4 * SC], BF16, ph1)
    y_sb = sb("y_sb", [128, NDC * SC], BF16, ph1)
    attn_sb = sb("attn_sb", [128, NCC * SC], BF16, ph1)
    state_sb = [sb(f"state{b}", [128, NDC * NST], F32, ph1) for b in range(B)]

    s.u("pool", "qgp", lambda: nc.gpsimd.dma_start(
        out=wxp_sb.rearrange("p (dk r) -> p dk r", dk=NDC),
        in_=wxp.rearrange("(dk p) r -> p dk r", p=128)), w=[bf("wxp")])
    s.u("pool", "qgp", lambda: nc.gpsimd.dma_start(out=wdt_sb, in_=wdt), w=[bf("wdt")])
    pc_f = [ep_x[:, i * 2048:(i + 1) * 2048] for i in range(2)]
    pc_b = [attn_sb[:, i * 2048:(i + 1) * 2048] for i in range(2)]
    pcj = [0]

    def precast(src_t, dst_t, rows, cols):
        for r0 in range(0, rows, 128):
            j = pcj[0] % 2
            pcj[0] += 1
            fB, bB = bf("ep_x"), bf("attn")
            s.u("sp", "qsp", lambda r0=r0, j=j: nc.sync.dma_start(
                out=pc_f[j][:, :cols], in_=src_t[r0:r0 + 128, :]), w=[fB],
                tag=f"pcl{r0}")
            s.u("act", "act", lambda j=j: nc.scalar.activation(
                pc_b[j][:, :cols], pc_f[j][:, :cols], AF.Copy), r=[fB], w=[bB],
                tag=f"pcc{r0}")
            s.u("sp", "qsp", lambda r0=r0, j=j: nc.sync.dma_start(
                out=dst_t[r0:r0 + 128, :], in_=pc_b[j][:, :cols]), r=[bB],
                w=[bf("wibf" if dst_t is wi_bf else "wobf")], tag=f"pcs{r0}")

    precast(wi, wi_bf, D, DI)
    precast(wo, wo_bf, DI, D)

    def front(b, tc):
        xl = xld[tc % 2]
        xlB = bf(f"xld{tc % 2}")
        t0 = tc * TC
        s.u("pool", "qgp", lambda xl=xl: nc.gpsimd.dma_start(
            out=xl.rearrange("p (cc t) -> p cc t", cc=NCC),
            in_=x_in[b, :, :, t0:t0 + TC].rearrange("cc p t -> p cc t")), w=[xlB], tag=f"ldx{b}_{tc}")
        srcs = [(xl[:, cc * TC:(cc + 1) * TC], xlB) for cc in range(NCC)]
        mbc, rbc, mB, rB = ln(srcs, TC, f"ln1_{b}_{tc}")
        for cc in range(NCC):
            ap = srcs[cc][0]
            s.u("dve", "dve", lambda ap=ap: nc.vector.tensor_sub(
                fin_sc[:, :TC], ap, mbc), r=[xlB, mB], w=[bf("fin")],
                tag=f"ln1a{b}_{tc}_{cc}")
            s.u("dve", "dve", lambda ap=ap: nc.vector.tensor_mul(
                ap, fin_sc[:, :TC], rbc), r=[bf("fin"), rB], w=[xlB],
                tag=f"ln1b{b}_{tc}_{cc}")
        hp = hpad[tc % 2]
        hpB = bf(f"hpad{tc % 2}")
        for gq in range(8):
            wr = wi_rb[gq % 2]
            wrB = bf(f"wi_rb{gq % 2}")
            s.u("sp", "qsp", lambda gq=gq, wr=wr: nc.sync.dma_start(
                out=wr.rearrange("p (kc dm) -> p kc dm", kc=8),
                in_=wi_bf[:, gq * 256:(gq + 1) * 256].rearrange("(kc p) dm -> p kc dm", p=128)),
                r=[bf("wibf")], w=[wrB], tag=f"ldwi{gq}_{b}_{tc}")
            for dcl in range(2):
                dc = gq * 2 + dcl
                bank = dc % 2
                ph = pb[bank][:, :TC]
                for kc in range(8):
                    s.u("pe", "pe", lambda ph=ph, wr=wr, kc=kc, dcl=dcl, xl=xl: nc.tensor.matmul(
                        ph, wr[:, kc * 256 + dcl * 128: kc * 256 + (dcl + 1) * 128],
                        xl[:, kc * TC:(kc + 1) * TC], start=(kc == 0), stop=(kc == 7)),
                        r=[wrB, xlB], w=[pbuf[bank]], tag=f"inproj{b}_{tc}_{dc}")
                s.u("dve", "dve", lambda ph=ph, dc=dc, hp=hp: nc.vector.tensor_scalar(
                    hp[:, dc * (TC + 2) + 2: dc * (TC + 2) + 2 + TC], ph,
                    cbcol(C_INB, dc), None, OP.add), r=[pbuf[bank], bf("cb")], w=[hpB],
                    tag=f"hevac{b}_{tc}_{dc}")
        hp3 = hp.rearrange("p (dc t) -> p dc t", dc=NDC)
        if tc > 0:
            lc3 = lcol2[b].rearrange("p (dc t) -> p dc t", dc=NDC)
            s.u("dve", "dve", lambda hp3=hp3, lc3=lc3: nc.vector.tensor_copy(
                hp3[:, :, 0:2], lc3), r=[bf(f"lcol{b}")], w=[hpB], tag=f"halo{b}_{tc}")
        else:
            s.u("dve", "dve", lambda hp3=hp3: nc.vector.memset(hp3[:, :, 0:2], 0.0),
                w=[hpB], tag=f"halo0{b}_{tc}")

    def conv_shifted(b, tc):
        hp = hpad[tc % 2]
        hpB = bf(f"hpad{tc % 2}")
        g = tc // 2
        j0 = 1 if tc == 0 else 0
        for dc in range(NDC):
            base = dc * (TC + 2)
            cv = cv_t[dc % 2]
            cvB = bf(f"cv_t{dc % 2}")
            nconv = TC - j0
            s.u("dve", "dve", lambda cv=cv, base=base, dc=dc, j0=j0, nconv=nconv, hp=hp:
                nc.vector.tensor_scalar(
                    cv[:, :nconv], hp[:, base + j0 + 1: base + j0 + 1 + nconv],
                    cbcol(C_CONVW, dc * 3 + 1), None, OP.mult),
                r=[hpB, bf("cb")], w=[cvB], tag=f"cv1_{b}_{tc}_{dc}")
            s.u("dve", "dve", lambda cv=cv, base=base, dc=dc, j0=j0, nconv=nconv, hp=hp:
                nc.vector.scalar_tensor_tensor(
                    cv[:, :nconv], hp[:, base + j0: base + j0 + nconv],
                    cbcol(C_CONVW, dc * 3 + 0), cv[:, :nconv], OP.mult, OP.add),
                r=[hpB, bf("cb")], w=[cvB], tag=f"cv0_{b}_{tc}_{dc}")
            s.u("dve", "dve", lambda cv=cv, base=base, dc=dc, j0=j0, nconv=nconv, hp=hp:
                nc.vector.scalar_tensor_tensor(
                    cv[:, :nconv], hp[:, base + j0 + 2: base + j0 + 2 + nconv],
                    cbcol(C_CONVW, dc * 3 + 2), cv[:, :nconv], OP.mult, OP.add),
                r=[hpB, bf("cb")], w=[cvB], tag=f"cv2_{b}_{tc}_{dc}")
            if tc == 0:
                usc = u_sc[0]
                uB = bf("u0")
                s.u("act", "act", lambda cv=cv, dc=dc, usc=usc: nc.scalar.activation(
                    usc[:, dc * SC: dc * SC + TC - 1], cv[:, :TC - 1], AF.Silu,
                    bias=cbcol(C_CONVB, dc)), r=[cvB, bf("cb")], w=[uB],
                    tag=f"silu{b}_{tc}_{dc}")
            elif tc % 2 == 0:
                up = u_sc[(g - 1) % 2]
                upB = bf(f"u{(g - 1) % 2}")
                uc = u_sc[g % 2]
                ucB = bf(f"u{g % 2}")
                s.u("act", "act", lambda cv=cv, dc=dc, up=up: nc.scalar.activation(
                    up[:, dc * SC + SC - 1: dc * SC + SC], cv[:, 0:1], AF.Silu,
                    bias=cbcol(C_CONVB, dc)), r=[cvB, bf("cb")], w=[upB],
                    tag=f"siluA{b}_{tc}_{dc}")
                s.u("act", "act", lambda cv=cv, dc=dc, uc=uc: nc.scalar.activation(
                    uc[:, dc * SC: dc * SC + TC - 1], cv[:, 1:TC], AF.Silu,
                    bias=cbcol(C_CONVB, dc)), r=[cvB, bf("cb")], w=[ucB],
                    tag=f"siluB{b}_{tc}_{dc}")
            else:
                uc = u_sc[g % 2]
                ucB = bf(f"u{g % 2}")
                s.u("act", "act", lambda cv=cv, dc=dc, uc=uc: nc.scalar.activation(
                    uc[:, dc * SC + TC - 1: dc * SC + 2 * TC - 1], cv, AF.Silu,
                    bias=cbcol(C_CONVB, dc)), r=[cvB, bf("cb")], w=[ucB],
                    tag=f"silu{b}_{tc}_{dc}")
            if tc == NTC - 1:
                uc = u_sc[(NSC - 1) % 2]
                ucB = bf(f"u{(NSC - 1) % 2}")
                s.u("dve", "dve", lambda cv=cv, base=base, dc=dc, hp=hp:
                    nc.vector.tensor_scalar(
                        cv[:, 0:1], hp[:, base + TC + 1: base + TC + 2],
                        cbcol(C_CONVW, dc * 3 + 1), None, OP.mult),
                    r=[hpB, bf("cb")], w=[cvB], tag=f"cvE1_{b}_{dc}")
                s.u("dve", "dve", lambda cv=cv, base=base, dc=dc, hp=hp:
                    nc.vector.scalar_tensor_tensor(
                        cv[:, 0:1], hp[:, base + TC: base + TC + 1],
                        cbcol(C_CONVW, dc * 3 + 0), cv[:, 0:1], OP.mult, OP.add),
                    r=[hpB, bf("cb")], w=[cvB], tag=f"cvE0_{b}_{dc}")
                s.u("act", "act", lambda cv=cv, dc=dc, uc=uc: nc.scalar.activation(
                    uc[:, dc * SC + SC - 1: dc * SC + SC], cv[:, 0:1], AF.Silu,
                    bias=cbcol(C_CONVB, dc)), r=[cvB, bf("cb")], w=[ucB],
                    tag=f"siluE{b}_{dc}")
        hp3 = hp.rearrange("p (dc t) -> p dc t", dc=NDC)
        lc3 = lcol2[b].rearrange("p (dc t) -> p dc t", dc=NDC)
        s.u("dve", "dve", lambda hp3=hp3, lc3=lc3: nc.vector.tensor_copy(
            lc3, hp3[:, :, TC:TC + 2]), r=[hpB], w=[bf(f"lcol{b}")],
            tag=f"lcap{b}_{tc}")

    def scan_group(b, g):
        usc = u_sc[g % 2]
        uB = bf(f"u{g % 2}")
        if g == 0:
            s.u("dve", "dve", lambda: nc.vector.memset(state_sb[b], 0.0),
                w=[bf(f"st{b}_{dcx}") for dcx in range(NDC)], tag=f"stinit{b}")
        pdl = pb[2][:64, :SC]
        pbm = pb[3][:16, :SC]
        for dc in range(NDC):
            s.u("pe", "pe", lambda dc=dc, usc=usc: nc.tensor.matmul(
                pdl, wxp_sb[:, dc * 80: dc * 80 + 64], usc[:, dc * SC:(dc + 1) * SC],
                start=(dc == 0), stop=(dc == NDC - 1)),
                r=[bf("wxp"), uB], w=[pbuf[2]], tag=f"xpd{b}_{g}_{dc}")
            s.u("pe", "pe", lambda dc=dc, usc=usc: nc.tensor.matmul(
                pbm, wxp_sb[:, dc * 80 + 64: dc * 80 + 80], usc[:, dc * SC:(dc + 1) * SC],
                start=(dc == 0), stop=(dc == NDC - 1)),
                r=[bf("wxp"), uB], w=[pbuf[3]], tag=f"xpb{b}_{g}_{dc}")
        dlb = dlr_bf[g % 2]
        dlbB = bf(f"dlr{g % 2}")
        s.u("dve", "dve", lambda dlb=dlb: nc.vector.tensor_scalar(
            dlb, pdl, cb_sb[:64, C_XPB:C_XPB + 1], None, OP.add),
            r=[pbuf[2], bf("cb")], w=[dlbB], tag=f"dlrevac{b}_{g}")
        bm = bm_sb[g % 2]
        bmB = bf(f"bm{g % 2}")
        s.u("dve", "dve", lambda bm=bm: nc.vector.tensor_scalar(
            bm, pbm, cb_sb[:16, C_XPBM:C_XPBM + 1], None, OP.add),
            r=[pbuf[3], bf("cb")], w=[bmB], tag=f"bmevac0{b}_{g}")
        for n in range(NST):
            bank = 2 + (n % 2)
            pn = pb[bank][:, :SC]
            s.u("pe", "pe", lambda pn=pn, n=n, bm=bm: nc.tensor.matmul(
                pn, oneh_sb[:, n * 128:(n + 1) * 128], bm, start=True, stop=True),
                r=[bmB, bf("oneh")], w=[pbuf[bank]], tag=f"bmbc{b}_{g}_{n}")
            s.u("act", "act", lambda pn=pn, n=n: nc.scalar.activation(
                bmbc[:, n * SC:(n + 1) * SC], pn, AF.Copy),
                r=[pbuf[bank]], w=[bf("bmbc")], tag=f"bmevac{b}_{g}_{n}")
        for dc in range(NDC):
            bank = 2 + (dc % 2)
            pdt = pb[bank][:, :SC]
            s.u("pe", "pe", lambda pdt=pdt, dc=dc, dlb=dlb: nc.tensor.matmul(
                pdt, wdt_sb[:, dc * 128:(dc + 1) * 128], dlb, start=True, stop=True),
                r=[bf("wdt"), dlbB], w=[pbuf[bank]], tag=f"dtmm{b}_{g}_{dc}")
            dl = delta_t[dc % 3]
            dlB = bf(f"delta{dc % 3}")
            sp_ = w_t[dc % 3]
            spB = bf(f"w_t{dc % 3}")
            s.u("dve", "dve", lambda pdt=pdt, dl=dl, dc=dc: nc.vector.tensor_scalar(
                dl, pdt, cbcol(C_DTB, dc), None, OP.add), r=[pbuf[bank], bf("cb")],
                w=[dlB], tag=f"spbias{b}_{g}_{dc}")
            s.u("dve", "dve", lambda dl=dl, sp_=sp_: nc.vector.scalar_tensor_tensor(
                sp_, dl, -1.0, dl, OP.mult, OP.max), r=[dlB], w=[spB],
                tag=f"spabs{b}_{g}_{dc}")
            s.u("act", "act", lambda sp_=sp_: nc.scalar.activation(sp_, sp_, AF.Exp, scale=-1.0),
                r=[spB, bf("cb")], w=[spB], tag=f"spexp{b}_{g}_{dc}")
            s.u("act", "act", lambda sp_=sp_: nc.scalar.activation(sp_, sp_, AF.Ln, bias=1.0),
                r=[spB, bf("cb")], w=[spB], tag=f"spln{b}_{g}_{dc}")
            s.u("dve", "dve", lambda sp_=sp_, dl=dl: nc.vector.scalar_tensor_tensor(
                dl, dl, 0.0, sp_, OP.max, OP.add), r=[spB], w=[dlB],
                tag=f"spfin{b}_{g}_{dc}")
            wt = w_t[dc % 3]
            wtB = bf(f"w_t{dc % 3}")
            s.u("pool", "pool", lambda dl=dl, wt=wt, dc=dc, usc=usc: nc.gpsimd.tensor_mul(
                wt, dl, usc[:, dc * SC:(dc + 1) * SC]), r=[dlB, uB], w=[wtB],
                tag=f"wmul{b}_{g}_{dc}")
            for half in range(2):
                for nn in range(8):
                    n = half * 8 + nn
                    da = dA_t[n % 3]
                    daB = bf(f"dA{n % 3}")
                    db_ = dBu_t[n % 3]
                    dbB = bf(f"dBu{n % 3}")
                    s.u("act", "act", lambda dl=dl, da=da, dc=dc, n=n: nc.scalar.activation(
                        da, dl, AF.Exp, scale=A_sb[:, dc * NST + n: dc * NST + n + 1]),
                        r=[dlB, bf("A")], w=[daB], tag=f"dA{b}_{g}_{dc}_{n}")
                    if n % 2 == 0:
                        s.u("dve", "dve", lambda wt=wt, db_=db_, n=n: nc.vector.tensor_mul(
                            db_, wt, bmbc[:, n * SC:(n + 1) * SC]), r=[wtB, bf("bmbc")],
                            w=[dbB], tag=f"dBu{b}_{g}_{dc}_{n}")
                    else:
                        s.u("pool", "pool", lambda wt=wt, db_=db_, n=n: nc.gpsimd.tensor_mul(
                            db_, wt, bmbc[:, n * SC:(n + 1) * SC]), r=[wtB, bf("bmbc")],
                            w=[dbB], tag=f"dBu{b}_{g}_{dc}_{n}")
                    s.u("dve", "dve", lambda da=da, db_=db_, dc=dc, n=n, nn=nn:
                        nc.vector.tensor_tensor_scan(
                            S_h[:, nn * SC:(nn + 1) * SC], da, db_,
                            state_sb[b][:, dc * NST + n: dc * NST + n + 1],
                            OP.mult, OP.add), r=[daB, dbB, bf(f"st{b}_{dc}")], w=[bf("S")],
                        tag=f"scan{b}_{g}_{dc}_{n}")
                s.u("pool", "pool", lambda dc=dc, half=half: nc.gpsimd.tensor_copy(
                    state_sb[b][:, dc * NST + half * 8: dc * NST + half * 8 + 8],
                    S_h.rearrange("p (n t) -> p n t", n=8)[:, :, SC - 1:SC]),
                    r=[bf("S")], w=[bf(f"st{b}_{dc}")], tag=f"stsave{b}_{g}_{dc}_{half}")
                rd = rd1 if half == 0 else rd2
                rdB = bf("rd1" if half == 0 else "rd2")
                for pr in range(4):
                    s.u("pool", "pool", lambda pr=pr: nc.gpsimd.tensor_add(
                        ytr[:, pr * SC:(pr + 1) * SC],
                        S_h[:, (2 * pr) * SC:(2 * pr + 1) * SC],
                        S_h[:, (2 * pr + 1) * SC:(2 * pr + 2) * SC]),
                        r=[bf("S")], w=[bf("ytr")], tag=f"yt{b}_{g}_{dc}_{half}_{pr}")
                s.u("dve", "dve", lambda rd=rd: nc.vector.tensor_add(
                    rd, ytr[:, 0:SC], ytr[:, SC:2 * SC]), r=[bf("ytr")], w=[rdB],
                    tag=f"yta{b}_{g}_{dc}_{half}")
                s.u("dve", "dve", lambda: nc.vector.tensor_add(
                    yred2, ytr[:, 2 * SC:3 * SC], ytr[:, 3 * SC:4 * SC]),
                    r=[bf("ytr")], w=[bf("yred2")], tag=f"ytb{b}_{g}_{dc}_{half}")
                s.u("dve", "dve", lambda rd=rd: nc.vector.tensor_add(
                    rd, rd, yred2), r=[bf("yred2")], w=[rdB],
                    tag=f"ytc{b}_{g}_{dc}_{half}")
            s.u("dve", "dve", lambda: nc.vector.tensor_add(rd1, rd1, rd2),
                r=[bf("rd2")], w=[bf("rd1")], tag=f"yadd{b}_{g}_{dc}")
            s.u("dve", "dve", lambda dc=dc, usc=usc: nc.vector.scalar_tensor_tensor(
                y_sb[:, dc * SC:(dc + 1) * SC], usc[:, dc * SC:(dc + 1) * SC],
                cbcol(C_D, dc), rd1, OP.mult, OP.add), r=[bf("rd1"), uB, bf("cb")],
                w=[bf("y")], tag=f"y2_{b}_{g}_{dc}")
            if DEBUG_MAMBA and b == 0 and g == 0 and dc == 0:
                for nm, ap_, rb_ in (("dbg_delta", dl, bf("delta0")),
                                     ("dbg_w", wt, bf("w_t0")),
                                     ("dbg_dA", dA_t[1], bf("dA1")),
                                     ("dbg_dBu", dBu_t[1], bf("dBu1")),
                                     ("dbg_S", S_h, bf("S"))):
                    s.u("sp", "qsp", lambda nm=nm, ap_=ap_: nc.sync.dma_start(
                        out=T[nm], in_=ap_), r=[rb_], tag="dump" + nm)

    def epilogue(b, g):
        if DEBUG_MAMBA and b == 0 and g == 0:
            for nm, ap_, rb_ in (("dbg_u", u_sc[0], bf("u0")),
                                 ("dbg_dlr", dlr_bf[0], bf("dlr0")),
                                 ("dbg_bm", bm_sb[0], bf("bm0")),
                                 ("dbg_bmbc", bmbc, bf("bmbc")),
                                 ("dbg_y", y_sb, bf("y"))):
                s.u("sp", "qsp", lambda nm=nm, ap_=ap_: nc.sync.dma_start(
                    out=T[nm], in_=ap_), r=[rb_], tag="dump" + nm)
        exB = bf("ep_x")
        s.u("sp", "qsp", lambda b=b, g=g: nc.sync.dma_start(
            out=ep_x.rearrange("p (cc t) -> p cc t", cc=NCC),
            in_=x_in[b, :, :, g * SC:(g + 1) * SC].rearrange("cc p t -> p cc t")), w=[exB], tag=f"ldepx{b}_{g}")
        for m in range(NCC):
            wr = wo_rb[m % 2]
            wrB = bf(f"wo_rb{m % 2}")
            s.u("sp", "qsp", lambda m=m, wr=wr: nc.sync.dma_start(
                out=wr.rearrange("p (dk mm) -> p dk mm", dk=NDC),
                in_=wo_bf[:, m * 128:(m + 1) * 128].rearrange("(dk p) mm -> p dk mm", p=128)),
                r=[bf("wobf")], w=[wrB], tag=f"ldwo{m}_{b}_{g}")
            bank = 4 + (m % 2)
            po = pb[bank][:, :SC]
            for dk in range(NDC):
                s.u("pe", "pe", lambda po=po, wr=wr, dk=dk: nc.tensor.matmul(
                    po, wr[:, dk * 128:(dk + 1) * 128], y_sb[:, dk * SC:(dk + 1) * SC],
                    start=(dk == 0), stop=(dk == NDC - 1)),
                    r=[wrB, bf("y")], w=[pbuf[bank]], tag=f"outproj{b}_{g}_{m}")
            s.u("dve", "dve", lambda po=po, m=m: nc.vector.tensor_scalar(
                attn_sb[:, m * SC:(m + 1) * SC], po, cbcol(C_OUTB, m), None, OP.add),
                r=[pbuf[bank], bf("cb")], w=[bf("attn")], tag=f"attnevac{b}_{g}_{m}")
        if DEBUG_MAMBA and b == 0 and g == 0:
            s.u("sp", "qsp", lambda: nc.sync.dma_start(
                out=T["dbg_attn"], in_=attn_sb), r=[bf("attn")], tag="dumpattn")
        srcs = [(attn_sb[:, cc * SC:(cc + 1) * SC], bf("attn")) for cc in range(NCC)]
        mbc, rbc, mB, rB = ln(srcs, SC, f"ln2_{b}_{g}")
        for cc in range(NCC):
            ap = srcs[cc][0]
            xcc = ep_x[:, cc * SC:(cc + 1) * SC]
            s.u("dve", "dve", lambda ap=ap: nc.vector.tensor_sub(
                fin_sc, ap, mbc), r=[bf("attn"), mB], w=[bf("fin")],
                tag=f"ln2a{b}_{g}_{cc}")
            s.u("dve", "dve", lambda: nc.vector.tensor_mul(
                fin_sc, fin_sc, rbc), r=[rB], w=[bf("fin")],
                tag=f"ln2b{b}_{g}_{cc}")
            s.u("dve", "dve", lambda xcc=xcc: nc.vector.tensor_add(
                xcc, xcc, fin_sc), r=[bf("fin")], w=[exB],
                tag=f"ln2c{b}_{g}_{cc}")
        s.u("sp", "qsp", lambda b=b, g=g: nc.sync.dma_start(
            out=x1_sp[b, :, :, g * SC:(g + 1) * SC].rearrange("cc p t -> p cc t"),
            in_=ep_x.rearrange("p (cc t) -> p cc t", cc=NCC)), r=[exB],
            w=[bf("x1_sp")], tag=f"stx1_{b}_{g}")
        srcs = [(ep_x[:, cc * SC:(cc + 1) * SC], exB) for cc in range(NCC)]
        mbc, rbc, mB, rB = ln(srcs, SC, f"ln3_{b}_{g}")
        for cc in range(NCC):
            ap = srcs[cc][0]
            dst = attn_sb[:, cc * SC:(cc + 1) * SC]
            s.u("dve", "dve", lambda ap=ap: nc.vector.tensor_sub(
                fin_sc, ap, mbc), r=[exB, mB], w=[bf("fin")],
                tag=f"ln3a{b}_{g}_{cc}")
            s.u("dve", "dve", lambda dst=dst: nc.vector.tensor_mul(
                dst, fin_sc, rbc), r=[bf("fin"), rB], w=[bf("attn")],
                tag=f"ln3b{b}_{g}_{cc}")
        s.u("sp", "qsp", lambda b=b, g=g: nc.sync.dma_start(
            out=xin1_sp[b, :, :, g * SC:(g + 1) * SC].rearrange("cc p t -> p cc t"),
            in_=attn_sb.rearrange("p (cc t) -> p cc t", cc=NCC)), r=[bf("attn")],
            w=[bf("xin1_sp")], tag=f"stxin1_{b}_{g}")

    def mamba():
        for b in range(B):
            for tc in range(NTC):
                front(b, tc)
                conv_shifted(b, tc)
                if tc >= 2 and tc % 2 == 0:
                    scan_group(b, tc // 2 - 1)
                    epilogue(b, tc // 2 - 1)
            scan_group(b, NSC - 1)
            epilogue(b, NSC - 1)

    if "mamba" in stages:
        mamba()
    ph1.close()
    s.fence()

    ph2 = contextlib.ExitStack()
    acc = sb("acc", [128, NCC * L], BF16, ph2)
    wm_rb = [sb(f"wm_rb{i}", [128, 8 * G * 128], BF16, ph2) for i in range(2)]
    wp_rb = [sb(f"wp_rb{i}", [128, G * 1024], BF16, ph2) for i in range(2)]
    xi_rb = [sb(f"xi_rb{i}", [128, NCC * TT], BF16, ph2) for i in range(3)]
    hbuf = sb("hbuf", [128, G * TT], BF16, ph2)
    wc1_sb = sb("wc1_sb", [128, 8 * 256], F32, ph2)
    wc2_rb = [sb(f"wc2_rb{i}", [128, 2 * 8 * 128], BF16, ph2) for i in range(2)]
    y1_bf = sb("y1_bf", [128, 4], BF16, ph2)
    ck_sb = sb("ck_sb", [128, 2 * 96], F32, ph2)
    gt_scr = sb("gt_scr", [128, 96], F32, ph2)
    gt_scr2 = sb("gt_scr2", [128, 96], F32, ph2)
    rank_sb = sb("rank_sb", [128, 96], F32, ph2)
    cbc_sb = sb("cbc_sb", [128, 96], F32, ph2)
    tpad_sb = sb("tpad_sb", [128, 128], F32, ph2)
    colrow = sb("colrow", [1, 4 * 96], F32, ph2)
    bis = {nm: [sb(f"bis_{nm}{b}", [128, 1],
                   mybir.dt.uint8 if nm == "sel" else F32, ph2) for b in range(B)]
           for nm in ("lo", "hi", "th", "cnt", "sel", "r")}
    fidx_sb = sb("fidx_sb", [128, 96], F32, ph2)
    ranki_sb = sb("ranki_sb", [128, 96], mybir.dt.int32, ph2)
    idxi_sb = {(k, b): sb(f"idxi{k}_{b}", [128, 32], mybir.dt.int32, ph2)
               for k in (2, 3) for b in range(B)}
    idxf_sb = sb("idxf_sb", [128, 32], F32, ph2)
    wmg_f32 = [sb(f"wmg{i}", [128, 1024], F32, ph2) for i in range(4)]
    wpg_f32 = [sb(f"wpg{i}", [128, 1024], F32, ph2) for i in range(4)]
    s.u("sp", "qsp", lambda: nc.sync.dma_start(out=fidx_sb, in_=fidx_c), w=[bf("fidx")])

    s.u("sp", "qsp", lambda: nc.sync.dma_start(
        out=wc1_sb.rearrange("p (kc m) -> p kc m", kc=8),
        in_=wc1.rearrange("(kc p) m -> p kc m", p=128)), w=[bf("wc1")])

    xi_ctr = [0]
    ind_toks = []
    bc_reg = nc.gpsimd.alloc_register("bcreg")
    s.add("pool", "pool", lambda: nc.gpsimd.reg_mov(bc_reg, ITER - 1), tag="bcreg")

    def ind_dep():
        return [ind_toks[-24]] if len(ind_toks) >= 24 else []

    def build_indices(k, b):
        mk = mask_sb[(1 if k == 2 else 3, b)]
        mkB = bf(f"mask{1 if k == 2 else 3}_{b}")
        p96 = pb[4][:1, :96]
        s.u("pe", "pe", lambda: nc.tensor.matmul(
            p96, ones_sb, mk, start=True, stop=True),
            r=[bf("ones"), mkB], w=[pbuf[4]], tag=f"gcol{k}_{b}")
        s.u("dve", "dve", lambda: nc.vector.tensor_tensor_scan(
            colrow[:, 288:384], ones1_sb[:1, :96], p96, 0.0, OP.mult, OP.add),
            r=[pbuf[4], bf("ones1")], w=[bf("crow")], tag=f"gcum{k}_{b}")
        s.u("dve", "dve", lambda: nc.vector.tensor_sub(
            colrow[:, 288:384], colrow[:, 288:384], p96), r=[pbuf[4]],
            w=[bf("crow")], tag=f"gex{k}_{b}")
        pc96 = pb[4][:, :96]
        s.u("pe", "pe", lambda: nc.tensor.matmul(
            pc96, ones1_sb, colrow[:1, 288:384], start=True, stop=True),
            r=[bf("crow"), bf("ones1")], w=[pbuf[4]], tag=f"gcbc{k}_{b}")
        s.u("dve", "dve", lambda: nc.vector.tensor_copy(cbc_sb, pc96),
            r=[pbuf[4]], w=[bf("cbc")], tag=f"gcbcc{k}_{b}")
        pp = pb[5][:, :96]
        s.u("pe", "pe", lambda: nc.tensor.matmul(
            pp, ltri_sb, mk, start=True, stop=True),
            r=[bf("ltri"), mkB], w=[pbuf[5]], tag=f"gpfx{k}_{b}")
        s.u("dve", "dve", lambda: nc.vector.tensor_add(rank_sb, cbc_sb, pp),
            r=[bf("cbc"), pbuf[5]], w=[bf("rank")], tag=f"grank{k}_{b}")
        s.u("dve", "dve", lambda: nc.vector.tensor_scalar(
            gt_scr, mk, -2.0e9, 2.0e9, OP.mult, OP.add), r=[mkB], w=[bf("gts")],
            tag=f"gskip{k}_{b}")
        s.u("dve", "dve", lambda: nc.vector.tensor_add(rank_sb, rank_sb, gt_scr),
            r=[bf("gts")], w=[bf("rank")], tag=f"gskip2{k}_{b}")
        s.u("dve", "dve", lambda: nc.vector.tensor_copy(ranki_sb, rank_sb),
            r=[bf("rank")], w=[bf("ranki")], tag=f"gcast{k}_{b}")
        for j in range(96):
            t = s.u("pool", "qix", lambda j=j, k=k, b=b: nc.gpsimd.indirect_dma_start(
                out=idx_sp[(k, b)],
                out_offset=bass.IndirectOffsetOnAxis(ap=ranki_sb[:, j:j + 1], axis=0),
                in_=fidx_sb[:, j:j + 1], in_offset=None,
                bounds_check=bc_reg, oob_is_err=False),
                r=[bf("ranki"), bf("fidx")], w=[bf(f"idxsp{k}_{b}")],
                deps=ind_dep(), tag=f"gsc{k}_{b}_{j}")
            ind_toks.append(t)
        s.u("sp", "qsp", lambda k=k, b=b: nc.sync.dma_start(
            out=idxf_sb.rearrange("p (g o) -> p g o", o=1),
            in_=idx_sp[(k, b)].rearrange("(p g) o -> p g o", g=32)),
            r=[bf(f"idxsp{k}_{b}")], w=[bf("idxf")], tag=f"gld{k}_{b}")
        s.u("dve", "dve", lambda k=k, b=b: nc.vector.tensor_copy(
            idxi_sb[(k, b)], idxf_sb), r=[bf("idxf")], w=[bf(f"idxi{k}_{b}")],
            tag=f"gldc{k}_{b}")

    def ffn_iter_b(k, b):
        xin_t = {1: xin1_sp, 2: xn_sp[1], 3: xin3_sp}[k]
        xinB = {1: bf("xin1_sp"), 2: bf(f"xn1_{b}"), 3: None}[k]
        nfc = 32
        ngr = nfc // G
        accB = bf("acc")
        if True:
            for fg in range(ngr):
                wmr = wm_rb[fg % 2]
                wmrB = bf(f"wm_rb{fg % 2}")
                wpr = wp_rb[fg % 2]
                wprB = bf(f"wp_rb{fg % 2}")
                if k == 1:
                    s.u("pool", "qgp", lambda fg=fg, wmr=wmr: nc.gpsimd.dma_start(
                        out=wmr.rearrange("p (kc fm) -> p kc fm", kc=8),
                        in_=wm[:, fg * G * 128:(fg + 1) * G * 128].rearrange(
                            "(kc p) fm -> p kc fm", p=128)),
                        w=[wmrB], tag=f"ldwm{k}_{b}_{fg}")
                    s.u("pool", "qgp", lambda fg=fg, wpr=wpr: nc.gpsimd.dma_start(
                        out=wpr.rearrange("p (f c) -> p f c", f=G),
                        in_=wp[fg * G * 128:(fg + 1) * G * 128, :].rearrange(
                            "(f p) c -> p f c", p=128)),
                        w=[wprB], tag=f"ldwp{k}_{b}_{fg}")
                else:
                    idxB = bf(f"idxi{k}_{b}")
                    for gch in range(G):
                        g = fg * G + gch
                        wg = wmg_f32[g % 4]
                        wgB = bf(f"wmg{g % 4}")
                        t = s.u("pool", "qgp", lambda g=g, wg=wg, k=k, b=b: nc.gpsimd.indirect_dma_start(
                            out=wg, out_offset=None, in_=wmR,
                            in_offset=bass.IndirectOffsetOnAxis(
                                ap=idxi_sb[(k, b)][:, g:g + 1], axis=0)),
                            r=[idxB], w=[wgB], deps=ind_dep(), tag=f"gwm{k}_{b}_{g}")
                        ind_toks.append(t)
                        for kc in range(NCC):
                            ptr = pb[6 + (kc % 2)][:, :128]
                            s.u("pe", "pe", lambda ptr=ptr, wg=wg, kc=kc: nc.tensor.transpose(
                                ptr, wg[:, kc * 128:(kc + 1) * 128], ident_sb),
                                r=[wgB, bf("ident")], w=[pbuf[6 + (kc % 2)]],
                                tag=f"gtr{k}_{b}_{g}_{kc}")
                            s.u("dve", "dve", lambda ptr=ptr, wmr=wmr, kc=kc, gch=gch: nc.vector.tensor_copy(
                                wmr[:, kc * G * 128 + gch * 128: kc * G * 128 + (gch + 1) * 128], ptr),
                                r=[pbuf[6 + (kc % 2)]], w=[wmrB],
                                tag=f"gtev{k}_{b}_{g}_{kc}")
                        wpg = wpg_f32[g % 4]
                        wpgB = bf(f"wpg{g % 4}")
                        t = s.u("pool", "qgp", lambda g=g, wpg=wpg, k=k, b=b: nc.gpsimd.indirect_dma_start(
                            out=wpg, out_offset=None, in_=wp,
                            in_offset=bass.IndirectOffsetOnAxis(
                                ap=idxi_sb[(k, b)][:, g:g + 1], axis=0)),
                            r=[idxB], w=[wpgB], deps=ind_dep(), tag=f"gwp{k}_{b}_{g}")
                        ind_toks.append(t)
                        s.u("act", "act", lambda wpg=wpg, wpr=wpr, gch=gch: nc.scalar.activation(
                            wpr[:, gch * 1024:(gch + 1) * 1024], wpg, AF.Copy),
                            r=[wpgB], w=[wprB], tag=f"gwpc{k}_{b}_{g}")
                for tt in range(L // TT):
                    slot = xi_ctr[0] % 3
                    xi_ctr[0] += 1
                    xi = xi_rb[slot]
                    xiB = bf(f"xi{slot}")
                    xinBt = bf(f"xin3_{b}_{tt}") if k == 3 else xinB
                    s.u("sp", "qsp", lambda b=b, tt=tt, xi=xi: nc.sync.dma_start(
                        out=xi.rearrange("p (cc t) -> p cc t", cc=NCC),
                        in_=xin_t[b, :, :, tt * TT:(tt + 1) * TT].rearrange("cc p t -> p cc t")),
                        r=[xinBt], w=[xiB], tag=f"ldxi{k}_{b}_{fg}_{tt}")
                    hB = bf("hbuf")
                    for fl in range(G):
                        f = fg * G + fl
                        bank = fl % 2
                        phh = pb[bank][:, :TT]
                        for kc in range(NCC):
                            s.u("pe", "pe", lambda phh=phh, wmr=wmr, kc=kc, fl=fl, xi=xi: nc.tensor.matmul(
                                phh, wmr[:, kc * G * 128 + fl * 128: kc * G * 128 + (fl + 1) * 128],
                                xi[:, kc * TT:(kc + 1) * TT], start=(kc == 0), stop=(kc == NCC - 1)),
                                r=[wmrB, xiB], w=[pbuf[bank]], tag=f"wm{k}_{b}_{fg}_{tt}_{fl}")
                        s.u("act", "act", lambda phh=phh, fl=fl: nc.scalar.activation(
                            hbuf[:, fl * TT:(fl + 1) * TT], phh, AF.Relu),
                            r=[pbuf[bank]], w=[hB], tag=f"hrelu{k}_{b}_{fg}_{tt}_{fl}")
                    if DEBUG_FFN and k == 1 and b == 0 and fg == 0 and tt == 0:
                        s.u("sp", "qsp", lambda: nc.sync.dma_start(
                            out=T["dbg_h"], in_=hbuf), r=[bf("hbuf")], tag="dumph")
                        s.u("sp", "qsp", lambda xi=xi: nc.sync.dma_start(
                            out=T["dbg_xi"], in_=xi), r=[bf(f"xi{(xi_ctr[0]-1) % 3}")], tag="dumpxi")
                    for cc in range(NCC):
                        bank = 2 + (cc % 2)
                        pxn = pb[bank][:, :TT]
                        for fl in range(G):
                            s.u("pe", "pe", lambda pxn=pxn, wpr=wpr, fl=fl, cc=cc: nc.tensor.matmul(
                                pxn, wpr[:, fl * 1024 + cc * 128: fl * 1024 + (cc + 1) * 128],
                                hbuf[:, fl * TT:(fl + 1) * TT], start=(fl == 0), stop=(fl == G - 1)),
                                r=[wprB, hB], w=[pbuf[bank]], tag=f"wp{k}_{b}_{fg}_{tt}_{cc}")
                        accv = acc[:, cc * L + tt * TT: cc * L + (tt + 1) * TT]
                        if fg == 0:
                            s.u("dve", "dve", lambda pxn=pxn, accv=accv: nc.vector.tensor_copy(
                                accv, pxn), r=[pbuf[bank]], w=[accB],
                                tag=f"acc0{k}_{b}_{fg}_{tt}_{cc}")
                        else:
                            s.u("dve", "dve", lambda pxn=pxn, accv=accv: nc.vector.tensor_add(
                                accv, accv, pxn), r=[pbuf[bank]], w=[accB],
                                tag=f"accp{k}_{b}_{fg}_{tt}_{cc}")
            if DEBUG_FFN and k == 1 and b == 0:
                s.u("sp", "qsp", lambda: nc.sync.dma_start(
                    out=T["dbg_acc"], in_=acc), r=[bf("acc")], tag="dumpacc")
        if True:
            for cc in range(NCC):
                s.u("dve", "dve", lambda cc=cc, b=b: nc.vector.tensor_scalar(
                    hbuf[:, :L], acc[:, cc * L:(cc + 1) * L], 1.0, 0.0, OP.mult, OP.add,
                    accum_out=pooled_sb[:, cc * 2 + b: cc * 2 + b + 1]),
                    r=[accB], w=[bf("hbuf"), bf("pooled")], tag=f"pool{k}_{b}_{cc}")
            s.u("sp", "qsp", lambda b=b: nc.sync.dma_start(
                out=xn_sp[k][b].rearrange("cc p t -> p cc t"), in_=acc.rearrange("p (cc t) -> p cc t", cc=NCC)),
                r=[accB], w=[bf(f"xn{k}_{b}")], tag=f"stxn{k}_{b}")

    def gating_pre(k, b):
        for m in range(2):
            py1 = pb[6 + m][:, :1]
            for kc in range(NCC):
                s.u("pe", "pe", lambda py1=py1, kc=kc, m=m, b=b: nc.tensor.matmul(
                    py1, wc1_sb[:, kc * 256 + m * 128: kc * 256 + (m + 1) * 128],
                    pooled_sb[:, kc * 2 + b: kc * 2 + b + 1], start=(kc == 0), stop=(kc == NCC - 1)),
                    r=[bf("wc1"), bf("pooled")], w=[pbuf[6 + m]], tag=f"y1mm{k}_{b}_{m}_{kc}")
            s.u("act", "act", lambda py1=py1, m=m, b=b: nc.scalar.activation(
                y1_bf[:, m * 2 + b: m * 2 + b + 1], py1, AF.Relu), r=[pbuf[6 + m]],
                w=[bf("y1")], tag=f"y1relu{k}_{b}_{m}")
        for piece in range(12):
            wcr = wc2_rb[piece % 2]
            wcrB = bf(f"wc2_rb{piece % 2}")
            if True:
                s.u("pool", "qgp", lambda piece=piece, wcr=wcr: nc.gpsimd.dma_start(
                    out=wcr.rearrange("p (m f) -> p m f", m=2),
                    in_=wc2[:, piece * 1024:(piece + 1) * 1024].rearrange(
                        "(m p) f -> p m f", p=128)), w=[wcrB], tag=f"ldwc2_{k}_{piece}")
            for fl in range(8):
                f = piece * 8 + fl
                bank = 6 + (f % 2)
                pck = pb[bank][:, :1]
                for m in range(2):
                    s.u("pe", "pe", lambda pck=pck, fl=fl, m=m, wcr=wcr, b=b: nc.tensor.matmul(
                        pck, wcr[:, m * 1024 + fl * 128: m * 1024 + (fl + 1) * 128],
                        y1_bf[:, m * 2 + b: m * 2 + b + 1], start=(m == 0), stop=(m == 1)),
                        r=[wcrB, bf("y1")], w=[pbuf[bank]], tag=f"ckmm{k}_{b}_{f}_{m}")
                s.u("dve", "dve", lambda pck=pck, f=f, b=b: nc.vector.tensor_copy(
                    ck_sb[:, b * 96 + f: b * 96 + f + 1], pck),
                    r=[pbuf[bank]], w=[bf("ck")], tag=f"ckevac{k}_{b}_{f}")
    def gating_row(k, b):
        if True:
            ckv = ck_sb[:, b * 96:(b + 1) * 96]
            s.u("dve", "dve", lambda ckv=ckv, b=b: nc.vector.reduce_max(
                bis["cnt"][b], ckv, AX.X, apply_absolute_value=True), r=[bf("ck")],
                w=[bf(f"bcnt{b}")], tag=f"bamax{k}_{b}")
            s.u("dve", "dve", lambda: nc.vector.memset(tpad_sb, NEG_BIG),
                w=[bf("tpad")], tag=f"bpad{k}_{b}")
            s.u("dve", "dve", lambda b=b: nc.vector.tensor_copy(
                tpad_sb[:, 0:1], bis["cnt"][b]), r=[bf(f"bcnt{b}")], w=[bf("tpad")],
                tag=f"bpadc{k}_{b}")
            ptr = pb[4][:, :128]
            s.u("pe", "pe", lambda ptr=ptr: nc.tensor.transpose(ptr, tpad_sb, ident_sb),
                r=[bf("tpad"), bf("ident")], w=[pbuf[4]], tag=f"btr{k}_{b}")
            s.u("dve", "dve", lambda ptr=ptr: nc.vector.reduce_max(
                colrow[:, 0:1], ptr[0:1, :], AX.X), r=[pbuf[4]], w=[bf("crow")],
                tag=f"bamax2{k}_{b}")
            pbc = pb[4][:, :1]
            s.u("pe", "pe", lambda pbc=pbc: nc.tensor.matmul(
                pbc, ones1_sb, colrow[:1, 0:1], start=True, stop=True),
                r=[bf("crow"), bf("ones1")], w=[pbuf[4]], tag=f"bhl{k}_{b}")
            s.u("dve", "dve", lambda b=b, pbc=pbc: nc.vector.tensor_copy(
                bis["hi"][b], pbc), r=[pbuf[4]], w=[bf(f"bhi{b}")], tag=f"bhi0{k}_{b}")
            s.u("dve", "dve", lambda b=b, pbc=pbc: nc.vector.tensor_scalar(
                bis["lo"][b], pbc, -1.0, -1.0, OP.mult, OP.add), r=[pbuf[4]],
                w=[bf(f"blo{b}")], tag=f"blo0{k}_{b}")
            s.u("dve", "dve", lambda b=b: nc.vector.tensor_add(
                bis["th"][b], bis["lo"][b], bis["hi"][b]), r=[bf(f"blo{b}"), bf(f"bhi{b}")],
                w=[bf(f"bth{b}")], tag=f"bth0{k}_{b}")
            s.u("dve", "dve", lambda b=b: nc.vector.tensor_scalar_mul(
                bis["th"][b], bis["th"][b], 0.5), w=[bf(f"bth{b}")], tag=f"bth00{k}_{b}")
            for it in range(BIS_ITERS):
                s.u("dve", "dve", lambda ckv=ckv, b=b: nc.vector.tensor_scalar(
                    gt_scr, ckv, bis["th"][b], 0.0, OP.is_gt, OP.add,
                    accum_out=bis["cnt"][b]), r=[bf("ck"), bf(f"bth{b}")],
                    w=[bf("gts"), bf(f"bcnt{b}")], tag=f"bcnt{k}_{b}_{it}")
                pc = pb[4][:1, :1]
                s.u("pe", "pe", lambda pc=pc, b=b: nc.tensor.matmul(
                    pc, ones_sb, bis["cnt"][b], start=True, stop=True),
                    r=[bf("ones"), bf(f"bcnt{b}")], w=[pbuf[4]], tag=f"bsum{k}_{b}_{it}")
                s.u("act", "act", lambda pc=pc: nc.scalar.activation(
                    scal_sb[:, 0:1], pc, AF.Copy), r=[pbuf[4]], w=[bf("scal")],
                    tag=f"bsc{k}_{b}_{it}")
                pb4 = pb[4][:, :1]
                s.u("pe", "pe", lambda pb4=pb4: nc.tensor.matmul(
                    pb4, ones1_sb, scal_sb[:1, 0:1], start=True, stop=True),
                    r=[bf("scal"), bf("ones1")], w=[pbuf[4]], tag=f"bbc{k}_{b}_{it}")
                s.u("dve", "dve", lambda pb4=pb4, b=b: nc.vector.tensor_scalar(
                    bis["sel"][b], pb4, float(ITER), None, OP.is_ge),
                    r=[pbuf[4]], w=[bf(f"bsel{b}")], tag=f"bsel{k}_{b}_{it}")
                s.u("dve", "dve", lambda b=b: nc.vector.select(
                    bis["lo"][b], bis["sel"][b], bis["th"][b], bis["lo"][b]),
                    r=[bf(f"bsel{b}"), bf(f"bth{b}")], w=[bf(f"blo{b}")],
                    tag=f"bsl{k}_{b}_{it}")
                s.u("dve", "dve", lambda b=b: nc.vector.select(
                    bis["hi"][b], bis["sel"][b], bis["hi"][b], bis["th"][b]),
                    r=[bf(f"bsel{b}"), bf(f"bth{b}")], w=[bf(f"bhi{b}")],
                    tag=f"bsh{k}_{b}_{it}")
                s.u("dve", "dve", lambda b=b: nc.vector.tensor_add(
                    bis["th"][b], bis["lo"][b], bis["hi"][b]),
                    r=[bf(f"blo{b}"), bf(f"bhi{b}")], w=[bf(f"bth{b}")],
                    tag=f"bt{k}_{b}_{it}")
                s.u("dve", "dve", lambda b=b: nc.vector.tensor_scalar_mul(
                    bis["th"][b], bis["th"][b], 0.5), w=[bf(f"bth{b}")],
                    tag=f"bt2{k}_{b}_{it}")
            s.u("dve", "dve", lambda ckv=ckv, b=b: nc.vector.tensor_scalar(
                gt_scr, ckv, bis["hi"][b], 0.0, OP.is_gt, OP.add, accum_out=bis["cnt"][b]),
                r=[bf("ck"), bf(f"bhi{b}")], w=[bf("gts"), bf(f"bcnt{b}")],
                tag=f"mgt{k}_{b}")
            s.u("dve", "dve", lambda ckv=ckv, b=b: nc.vector.tensor_scalar(
                gt_scr2, ckv, bis["lo"][b], None, OP.is_gt), r=[bf("ck"), bf(f"blo{b}")],
                w=[bf("gts2")], tag=f"mge{k}_{b}")
            s.u("dve", "dve", lambda: nc.vector.tensor_sub(gt_scr2, gt_scr2, gt_scr),
                r=[bf("gts")], w=[bf("gts2")], tag=f"minT{k}_{b}")
            pc = pb[4][:1, :1]
            s.u("pe", "pe", lambda pc=pc, b=b: nc.tensor.matmul(
                pc, ones_sb, bis["cnt"][b], start=True, stop=True),
                r=[bf("ones"), bf(f"bcnt{b}")], w=[pbuf[4]], tag=f"msum{k}_{b}")
            s.u("act", "act", lambda pc=pc: nc.scalar.activation(
                scal_sb[:, 1:2], pc, AF.Copy), r=[pbuf[4]], w=[bf("scal")],
                tag=f"msc{k}_{b}")
            pb4 = pb[4][:, :1]
            s.u("pe", "pe", lambda pb4=pb4: nc.tensor.matmul(
                pb4, ones1_sb, scal_sb[:1, 1:2], start=True, stop=True),
                r=[bf("scal"), bf("ones1")], w=[pbuf[4]], tag=f"mbc{k}_{b}")
            s.u("dve", "dve", lambda pb4=pb4, b=b: nc.vector.tensor_scalar(
                bis["r"][b], pb4, -1.0, float(ITER), OP.mult, OP.add),
                r=[pbuf[4]], w=[bf(f"br{b}")], tag=f"mr{k}_{b}")
            p96 = pb[4][:1, :96]
            s.u("pe", "pe", lambda p96=p96: nc.tensor.matmul(
                p96, ones_sb, gt_scr2, start=True, stop=True),
                r=[bf("ones"), bf("gts2")], w=[pbuf[4]], tag=f"ccol{k}_{b}")
            s.u("dve", "dve", lambda p96=p96: nc.vector.tensor_tensor_scan(
                colrow[:, 96:192], ones1_sb[:1, :96], p96, 0.0, OP.mult, OP.add),
                r=[pbuf[4], bf("ones1")], w=[bf("crow")], tag=f"ccum{k}_{b}")
            s.u("dve", "dve", lambda p96=p96: nc.vector.tensor_sub(
                colrow[:, 96:192], colrow[:, 96:192], p96), r=[pbuf[4]],
                w=[bf("crow")], tag=f"cex{k}_{b}")
            pc96 = pb[4][:, :96]
            s.u("pe", "pe", lambda pc96=pc96: nc.tensor.matmul(
                pc96, ones1_sb, colrow[:1, 96:192], start=True, stop=True),
                r=[bf("crow"), bf("ones1")], w=[pbuf[4]], tag=f"cbc{k}_{b}")
            s.u("dve", "dve", lambda pc96=pc96: nc.vector.tensor_copy(cbc_sb, pc96),
                r=[pbuf[4]], w=[bf("cbc")], tag=f"cbcc{k}_{b}")
            pp = pb[5][:, :96]
            s.u("pe", "pe", lambda pp=pp: nc.tensor.matmul(
                pp, ltri_sb, gt_scr2, start=True, stop=True),
                r=[bf("ltri"), bf("gts2")], w=[pbuf[5]], tag=f"pfx{k}_{b}")
            s.u("dve", "dve", lambda pp=pp: nc.vector.tensor_add(rank_sb, cbc_sb, pp),
                r=[bf("cbc"), pbuf[5]], w=[bf("rank")], tag=f"rank{k}_{b}")
            s.u("dve", "dve", lambda b=b: nc.vector.tensor_scalar(
                rank_sb, rank_sb, bis["r"][b], None, OP.is_lt), r=[bf(f"br{b}")],
                w=[bf("rank")], tag=f"fill{k}_{b}")
            s.u("dve", "dve", lambda: nc.vector.tensor_mul(rank_sb, rank_sb, gt_scr2),
                r=[bf("gts2")], w=[bf("rank")], tag=f"fill2{k}_{b}")
            mk = mask_sb[(k, b)]
            mkB = bf(f"mask{k}_{b}")
            s.u("dve", "dve", lambda mk=mk: nc.vector.tensor_add(mk, gt_scr, rank_sb),
                r=[bf("gts"), bf("rank")], w=[mkB], tag=f"mk{k}_{b}")
            hist = bhist_sb if k == 1 else hist2_sb[b]
            histB = bf("bhist") if k == 1 else bf(f"hist2_{b}")
            s.u("dve", "dve", lambda mk=mk, hist=hist: nc.vector.tensor_mul(
                gt_scr, mk, hist), r=[mkB, histB], w=[bf("gts")], tag=f"ovl{k}_{b}")
            s.u("dve", "dve", lambda b=b: nc.vector.tensor_scalar(
                gt_scr, gt_scr, 1.0, 0.0, OP.mult, OP.add, accum_out=bis["cnt"][b]),
                w=[bf("gts"), bf(f"bcnt{b}")], tag=f"ovls{k}_{b}")
            pc = pb[4][:1, :1]
            s.u("pe", "pe", lambda pc=pc, b=b: nc.tensor.matmul(
                pc, ones_sb, bis["cnt"][b], start=True, stop=True),
                r=[bf("ones"), bf(f"bcnt{b}")], w=[pbuf[4]], tag=f"ovlm{k}_{b}")
            s.u("act", "act", lambda pc=pc, b=b: nc.scalar.activation(
                scal_sb[:, 4 + b:5 + b], pc, AF.Copy), r=[pbuf[4]], w=[bf("scal")],
                tag=f"ovle{k}_{b}")
            if k == 1:
                s.u("dve", "dve", lambda mk=mk, b=b: nc.vector.tensor_max(
                    hist2_sb[b], bhist_sb, mk), r=[mkB, bf("bhist")],
                    w=[bf(f"hist2_{b}")], tag=f"hist2_{b}")
    def gating_cc(k):
        s.u("dve", "dve", lambda: nc.vector.tensor_add(
            scal_sb[:, 6:7], scal_sb[:, 4:5], scal_sb[:, 5:6]), w=[bf("scal")],
            tag=f"ovladd{k}")
        s.u("dve", "dve", lambda: nc.vector.memset(scal_sb[:, 8:16], 0.0),
            w=[bf("scal")], tag=f"ccz{k}")
        s.u("dve", "dve", lambda: nc.vector.tensor_copy(
            scal_sb[:, 8:9], scal_sb[:, 6:7]), w=[bf("scal")], tag=f"ccc{k}")
        s.u("pool", "qgp", lambda: nc.gpsimd.dma_start(
            out=cc_in[k], in_=scal_sb[:1, 8:16]), r=[bf("scal")], w=[bf(f"cci{k}")],
            tag=f"ccw{k}")
        s.u("pool", "qcc", lambda: nc.gpsimd.collective_compute(
            "AllReduce", OP.add, replica_groups=[[i for i in range(8)]],
            ins=[cc_in[k].opt()], outs=[cc_out[k].opt()]), r=[bf(f"cci{k}")], w=[bf(f"cco{k}")],
            tag=f"ccar{k}")
        s.u("pool", "qgp", lambda: nc.gpsimd.dma_start(
            out=scal_sb[:1, 8:16], in_=cc_out[k]), r=[bf(f"cco{k}")], w=[bf("scal")],
            tag=f"ccr{k}")
        s.u("dve", "dve", lambda: nc.vector.tensor_scalar(
            scal_sb[:, 1 + k: 2 + k], scal_sb[:, 8:9], OVL_TH, None, OP.is_gt),
            w=[bf("scal")], tag=f"dk{k}")

    def bcast_d(nm, col):
        pb4 = pb[4][:, :1]
        s.u("pe", "pe", lambda pb4=pb4, col=col: nc.tensor.matmul(
            pb4, ones1_sb, scal_sb[:1, col:col + 1], start=True, stop=True),
            r=[bf("scal"), bf("ones1")], w=[pbuf[4]], tag=f"dbc{nm}")
        s.u("dve", "dve", lambda pb4=pb4, nm=nm: nc.vector.tensor_copy(
            d_bc[nm], pb4), r=[pbuf[4]], w=[bf(f"dbc{nm}")], tag=f"dbcc{nm}")

    def post_d3():
        s.u("dve", "dve", lambda: nc.vector.tensor_max(
            scal_sb[:, 3:4], scal_sb[:, 2:3], scal_sb[:, 3:4]), w=[bf("scal")],
            tag="d3max")
        bcast_d("d3", 3)

    def post_mask3_b(b):
        m1, m2, m3 = mask_sb[(1, b)], mask_sb[(2, b)], mask_sb[(3, b)]
        s.u("dve", "dve", lambda m1=m1, m2=m2: nc.vector.tensor_sub(
            gt_scr, m1, m2), r=[bf(f"mask1_{b}"), bf(f"mask2_{b}")], w=[bf("gts")],
            tag=f"m3a{b}")
        s.u("dve", "dve", lambda: nc.vector.tensor_scalar(
            gt_scr, gt_scr, d_bc["d2"], None, OP.mult), r=[bf("dbcd2")],
            w=[bf("gts")], tag=f"m3b{b}")
        s.u("dve", "dve", lambda m2=m2, m3=m3, b=b: nc.vector.tensor_add(
            m3, m2, gt_scr), r=[bf("gts"), bf(f"mask2_{b}")], w=[bf(f"mask3_{b}")],
            tag=f"m3c{b}")

    def make_xin3_b(b):
        for tt in range(L // TT):
            x1t, x2t, x3t = xi_rb[0], xi_rb[1], xi_rb[2]
            s.u("sp", "qsp", lambda x1t=x1t, b=b, tt=tt: nc.sync.dma_start(
                out=x1t.rearrange("p (cc t) -> p cc t", cc=NCC),
                in_=xn_sp[1][b, :, :, tt * TT:(tt + 1) * TT].rearrange("cc p t -> p cc t")), r=[bf(f"xn1_{b}")],
                w=[bf("xi0")], tag=f"x3l1{b}_{tt}")
            s.u("sp", "qsp", lambda x2t=x2t, b=b, tt=tt: nc.sync.dma_start(
                out=x2t.rearrange("p (cc t) -> p cc t", cc=NCC),
                in_=xn_sp[2][b, :, :, tt * TT:(tt + 1) * TT].rearrange("cc p t -> p cc t")), r=[bf(f"xn2_{b}")],
                w=[bf("xi1")], tag=f"x3l2{b}_{tt}")
            for cc in range(NCC):
                s.u("dve", "dve", lambda x1t=x1t, x2t=x2t, cc=cc: nc.vector.tensor_sub(
                    fin_sc, x1t[:, cc * TT:(cc + 1) * TT], x2t[:, cc * TT:(cc + 1) * TT]),
                    r=[bf("xi0"), bf("xi1")], w=[bf("fin")], tag=f"x3s{b}_{tt}_{cc}")
                s.u("dve", "dve", lambda x2t=x2t, x3t=x3t, cc=cc: nc.vector.scalar_tensor_tensor(
                    x3t[:, cc * TT:(cc + 1) * TT], fin_sc, d_bc["d2"],
                    x2t[:, cc * TT:(cc + 1) * TT], OP.mult, OP.add),
                    r=[bf("fin"), bf("xi1"), bf("dbcd2")], w=[bf("xi2")],
                    tag=f"x3b{b}_{tt}_{cc}")
            s.u("act", "qact", lambda x3t=x3t, b=b, tt=tt: nc.scalar.dma_start(
                out=xin3_sp[b, :, :, tt * TT:(tt + 1) * TT].rearrange("cc p t -> p cc t"),
                in_=x3t.rearrange("p (cc t) -> p cc t", cc=NCC)), r=[bf("xi2")],
                w=[bf(f"xin3_{b}_{tt}")], tag=f"x3w{b}_{tt}")

    if "ffn" in stages:
        ffn_iter_b(1, 0)
        if 2 in DEBUG_ITERS:
            gating_pre(1, 0)
            gating_row(1, 0)
            build_indices(2, 0)
            ffn_iter_b(1, 1)
            gating_pre(1, 1)
            gating_row(1, 1)
            ffn_iter_b(2, 0)
            build_indices(2, 1)
            gating_cc(1)
            bcast_d("d2", 2)
            ffn_iter_b(2, 1)
            gating_pre(2, 0)
            gating_row(2, 0)
            post_mask3_b(0)
            build_indices(3, 0)
            gating_pre(2, 1)
            gating_row(2, 1)
            post_mask3_b(1)
            build_indices(3, 1)
            make_xin3_b(0)
            make_xin3_b(1)
            ffn_iter_b(3, 0)
            gating_cc(2)
            post_d3()
            ffn_iter_b(3, 1)
        else:
            ffn_iter_b(1, 1)
    ph2.close()
    s.fence()

    ph3 = contextlib.ExitStack()
    f_a2 = [sb(f"f_a{i}", [128, NCC * TT], BF16, ph3) for i in range(2)]
    f_b2 = [sb(f"f_b{i}", [128, NCC * TT], BF16, ph3) for i in range(2)]
    f_x1 = [sb(f"f_x1{i}", [128, NCC * TT], F32, ph3) for i in range(2)]
    f_fin = [sb(f"f_fin{i}", [128, NCC * TT], F32, ph3) for i in range(2)]
    f_o = [sb(f"f_o{i}", [128, NCC * TT], F32, ph3) for i in range(2)]

    def final():
        out_toks = []
        for b in range(B):
            for tt in range(L // TT):
                sl = tt % 2
                ff = f_fin[sl]
                ffB = bf(f"f_fin{sl}")
                fx = f_x1[sl]
                fxB = bf(f"f_x1{sl}")
                fo = f_o[sl]
                foB = bf(f"f_o{sl}")
                f_a = f_a2[sl]
                faB = bf(f"f_a{sl}")
                f_b = f_b2[sl]
                fbB = bf(f"f_b{sl}")
                s.u("sp", "qsp", lambda b=b, tt=tt, f_a=f_a: nc.sync.dma_start(
                    out=f_a.rearrange("p (cc t) -> p cc t", cc=NCC),
                    in_=xin3_sp[b, :, :, tt * TT:(tt + 1) * TT].rearrange("cc p t -> p cc t")),
                    r=[bf(f"xin3_{b}_{tt}")], w=[faB], tag=f"fl1{b}_{tt}")
                s.u("act", "qact", lambda b=b, tt=tt, f_b=f_b: nc.scalar.dma_start(
                    out=f_b.rearrange("p (cc t) -> p cc t", cc=NCC),
                    in_=xn_sp[3][b, :, :, tt * TT:(tt + 1) * TT].rearrange("cc p t -> p cc t")),
                    r=[bf(f"xn3_{b}")], w=[fbB], tag=f"fl2{b}_{tt}")
                s.u("sp", "qsp", lambda b=b, tt=tt, fx=fx: nc.sync.dma_start(
                    out=fx.rearrange("p (cc t) -> p cc t", cc=NCC),
                    in_=x1_sp[b, :, :, tt * TT:(tt + 1) * TT].rearrange("cc p t -> p cc t")),
                    r=[bf("x1_sp")], w=[fxB], tag=f"fl3{b}_{tt}")
                for cc in range(NCC):
                    s.u("dve", "dve", lambda ff=ff, cc=cc, f_a=f_a, f_b=f_b: nc.vector.tensor_sub(
                        ff[:, cc * TT:(cc + 1) * TT], f_a[:, cc * TT:(cc + 1) * TT],
                        f_b[:, cc * TT:(cc + 1) * TT]),
                        r=[faB, fbB], w=[ffB], tag=f"fs{b}_{tt}_{cc}")
                    s.u("dve", "dve", lambda ff=ff, cc=cc, f_b=f_b: nc.vector.scalar_tensor_tensor(
                        ff[:, cc * TT:(cc + 1) * TT], ff[:, cc * TT:(cc + 1) * TT],
                        d_bc["d3"], f_b[:, cc * TT:(cc + 1) * TT], OP.mult, OP.add),
                        r=[fbB, bf("dbcd3")], w=[ffB], tag=f"fb{b}_{tt}_{cc}")
                srcs = [(ff[:, cc * TT:(cc + 1) * TT], ffB) for cc in range(NCC)]
                mbc, rbc, mB, rB = ln(srcs, TT, f"ln4_{b}_{tt}")
                for cc in range(NCC):
                    ap = srcs[cc][0]
                    s.u("dve", "dve", lambda ap=ap, cc=cc, fo=fo: nc.vector.tensor_sub(
                        fo[:, cc * TT:(cc + 1) * TT], ap, mbc), r=[ffB, mB], w=[foB],
                        tag=f"f4a{b}_{tt}_{cc}")
                    s.u("dve", "dve", lambda cc=cc, fo=fo: nc.vector.tensor_mul(
                        fo[:, cc * TT:(cc + 1) * TT], fo[:, cc * TT:(cc + 1) * TT], rbc),
                        r=[rB], w=[foB], tag=f"f4b{b}_{tt}_{cc}")
                    s.u("dve", "dve", lambda cc=cc, fo=fo, fx=fx: nc.vector.tensor_add(
                        fo[:, cc * TT:(cc + 1) * TT], fo[:, cc * TT:(cc + 1) * TT],
                        fx[:, cc * TT:(cc + 1) * TT]), r=[fxB], w=[foB],
                        tag=f"f4c{b}_{tt}_{cc}")
                t = s.u("sp", "qsp", lambda b=b, tt=tt, fo=fo: nc.sync.dma_start(
                    out=out[b, :, :, tt * TT:(tt + 1) * TT].rearrange("cc p t -> p cc t"),
                    in_=fo.rearrange("p (cc t) -> p cc t", cc=NCC)), r=[foB],
                    tag=f"fw{b}_{tt}")
                out_toks.append(t)
        s.final_deps.extend(out_toks)

    if "final" in stages:
        final()
    ph3.close()

    s.emit_all()
    return T


def _consts():
    f32 = np.float32
    ltri = np.zeros((128, 128), f32)
    for p in range(128):
        ltri[:p, p] = 1.0
    bhist = np.zeros((128, 96), f32)
    bhist[:, :32] = 1.0
    oneh = np.zeros((16, 16 * 128), f32)
    for n in range(16):
        oneh[n, n * 128:(n + 1) * 128] = 1.0
    return ltri, bhist, oneh


def _prep_core(ci, inp):
    f32 = np.float32
    x = inp["L0_attn_x"][2 * ci: 2 * ci + 2]
    xr = np.ascontiguousarray(x.transpose(0, 2, 1)).reshape(B, NCC, 128, L)
    cb = np.zeros((128, NCB), f32)
    cb[:, C_INB:C_INB + 16] = inp["L0_attn_in_b"].reshape(16, 128).T
    cb[:, C_DTB:C_DTB + 16] = inp["L0_attn_dt_b"].reshape(16, 128).T
    cb[:, C_CONVW:C_CONVW + 48] = np.asarray(inp["L0_attn_conv_w"]).reshape(16, 128, 3).transpose(1, 0, 2).reshape(128, 48)
    cb[:, C_CONVB:C_CONVB + 16] = inp["L0_attn_conv_b"].reshape(16, 128).T
    cb[:, C_D:C_D + 16] = inp["L0_attn_D_param"].reshape(16, 128).T
    cb[:, C_OUTB:C_OUTB + 8] = inp["L0_attn_out_b"].reshape(8, 128).T
    cb[:64, C_XPB] = inp["L0_attn_x_proj_b"][:64]
    cb[:16, C_XPBM] = inp["L0_attn_x_proj_b"][64:]
    for i, v in enumerate(K_VALS):
        cb[:, C_K + i] = v
    cb[:, C_ALOG:] = np.asarray(inp["L0_attn_A_log"]).reshape(16, 128, 16).transpose(1, 0, 2).reshape(128, 256)
    ltri, bhist, oneh = _consts()
    return dict(
        x=np.ascontiguousarray(xr, f32),
        wi=np.ascontiguousarray(np.asarray(inp["L0_attn_in_w"]).T, f32),
        wo=np.ascontiguousarray(np.asarray(inp["L0_attn_out_w"]).T, f32),
        wxp=np.ascontiguousarray(np.asarray(inp["L0_attn_x_proj_w"]).T, f32),
        wdt=np.ascontiguousarray(np.asarray(inp["L0_attn_dt_w"]).T, f32),
        wm=np.ascontiguousarray(np.asarray(inp["L0_ffn_Wm"]).T, f32),
        wmR=np.ascontiguousarray(np.asarray(inp["L0_ffn_Wm"]), f32),
        fidx_c=(np.arange(96, dtype=f32)[None, :] * 128 + np.arange(128, dtype=f32)[:, None]),
        wp=np.ascontiguousarray(np.asarray(inp["L0_ffn_Wp"]).T, f32),
        wc1=np.ascontiguousarray(np.asarray(inp["L0_ffn_Wc1"]).T, f32),
        wc2=np.ascontiguousarray(np.asarray(inp["L0_ffn_Wc2"]).T, f32),
        cb=cb,
        ones_c=np.ones((128, 128), f32),
        onescol_c=np.ones((128, 1), f32),
        onescolb_c=np.ones((128, 1), ml_dtypes.bfloat16),
        ident_c=np.eye(128, dtype=f32),
        ltri_c=ltri,
        bhist_c=bhist,
        oneh_c=oneh,
    )


def kernel(**inputs):
    inputs = {k: np.asarray(v) for k, v in inputs.items()}
    nc = bass.Bass("TRN2", target_bir_lowering=False, debug=False)
    build(nc)
    in_maps = [_prep_core(ci, inputs) for ci in range(8)]
    res = run_bass_kernel_spmd(nc, in_maps, core_ids=list(range(8)))
    outs = []
    for ci in range(8):
        o = np.asarray(res.results[ci]["out"])
        outs.append(o.reshape(B, D, L).transpose(0, 2, 1))
    return np.ascontiguousarray(np.concatenate(outs, axis=0), np.float32)


# version 23
# speedup vs baseline: 1.1144x; 1.1144x over previous
import contextlib
import numpy as np
import ml_dtypes
import concourse.bass as bass
import concourse.mybir as mybir
from concourse.bass_utils import run_bass_kernel_spmd

F32 = mybir.dt.float32
BF16 = mybir.dt.bfloat16
AF = mybir.ActivationFunctionType
OP = mybir.AluOpType
AX = mybir.AxisListType

B = 2
L = 2048
D = 1024
DI = 2048
NST = 16
RK = 64
FF = 12288
ITER = 4096
TC = 256
SC = 512
NTC = L // TC
NSC = L // SC
TT = 512
G = 8
NCC = D // 128
NDC = DI // 128
NFC = FF // 128
EPS = 1e-5
OVL_TH = float(np.float64(65536.0) - np.float64(np.float32(0.7)) * 65536.0)
BIS_ITERS = 36
NEG_BIG = -3.0e38

STAGES = ("mamba", "ffn", "final")
PARANOID = False
DEBUG_SPILL = False
DEBUG_MAMBA = False
DEBUG_FFN = False
DEBUG_FFN_ONLY = False
DEBUG_ITERS = (1, 2, 3)


class Buf:
    __slots__ = ("w", "rs")
    def __init__(self):
        self.w = None
        self.rs = []


class Op:
    __slots__ = ("sec", "stream", "emit", "deps", "inc", "idx", "tag")
    def __init__(self, sec, stream, emit, deps, idx, tag):
        self.sec = sec; self.stream = stream; self.emit = emit
        self.deps = deps; self.idx = idx; self.tag = tag; self.inc = 0


class Sched:
    NSUB = {"qsp": 4, "qgp": 8, "qact": 4, "qix": 16}
    SECTIONS = ("pe", "act", "dve", "pool", "sp")
    STREAMS = tuple(["pe", "act", "dve", "pool", "qcc"]
                    + [f"qsp{i}" for i in range(4)] + [f"qgp{i}" for i in range(8)]
                    + [f"qact{i}" for i in range(4)] + [f"qix{i}" for i in range(16)])

    def __init__(self, nc):
        self.nc = nc
        self.ops = []
        self.final_deps = []
        self._pending_fence = None
        self._qctr = {}
        self._qlast = {}

    def add(self, sec, stream, emit, deps=(), tag=""):
        deps = [d for d in deps if d is not None]
        if stream in self.NSUB:
            i = self._qctr.get(stream, 0)
            self._qctr[stream] = i + 1
            sub = f"{stream}{i % self.NSUB[stream]}"
            prev = self._qlast.get(sub)
            if prev is not None:
                deps = deps + [prev]
            stream = sub
        if self._pending_fence is not None and stream in self._pending_fence:
            deps = deps + self._pending_fence.pop(stream)
        op = Op(sec, stream, emit, sorted(set(deps)), len(self.ops), tag)
        self.ops.append(op)
        if stream.startswith("q") and stream != "qcc":
            self._qlast[stream] = op.idx
        return op.idx

    def u(self, sec, stream, emit, r=(), w=(), deps=(), tag=""):
        d = list(deps)
        for b in r:
            if b.w is not None:
                d.append(b.w)
        for b in w:
            if b.w is not None:
                d.append(b.w)
            d.extend(b.rs)
        t = self.add(sec, stream, emit, d, tag)
        for b in r:
            b.rs.append(t)
        for b in w:
            b.w = t
            b.rs = []
        return t

    def fence(self):
        last = {}
        for op in self.ops:
            last[op.stream] = op.idx
        toks = list(last.values())
        self._pending_fence = {st: list(toks) for st in self.STREAMS}

    def emit_all(self):
        nc = self.nc
        tgt = set()
        for op in self.ops:
            tgt.update(op.deps)
        if PARANOID:
            tgt.update(range(len(self.ops)))
        tgt.update(self.final_deps)
        last_of = {}
        for op in self.ops:
            last_of[op.stream] = op.idx
        tgt.update(last_of.values())
        val = {st: 0 for st in self.STREAMS}
        opval = [0] * len(self.ops)
        for op in self.ops:
            if op.idx in tgt or op.stream.startswith("q"):
                op.inc = 16 if (op.stream.startswith("q") and op.stream != "qcc") else 1
                val[op.stream] += op.inc
            opval[op.idx] = val[op.stream]
        self._opval = opval

        eng_of = {"pe": nc.tensor, "act": nc.scalar, "dve": nc.vector,
                  "pool": nc.gpsimd, "sp": nc.sync}
        with contextlib.ExitStack() as st:
            sems = {x: st.enter_context(nc.semaphore("sem_" + x))
                    for x in self.STREAMS}
            block = st.enter_context(nc.Block())

            def emit_section(sec):
                eng = eng_of[sec]
                known = {x: 0 for x in self.STREAMS}
                for op in self.ops:
                    if op.sec != sec:
                        continue
                    for d in op.deps:
                        dop = self.ops[d]
                        if (not PARANOID) and dop.sec == sec and not dop.stream.startswith("q"):
                            continue
                        v = self._opval[d]
                        if known[dop.stream] >= v:
                            continue
                        eng.wait_ge(sems[dop.stream], v)
                        known[dop.stream] = v
                    try:
                        inst = op.emit() if op.emit is not None else None
                    except Exception as e:
                        raise RuntimeError(f"emit failed op={op.idx} tag={op.tag}: {e}") from e
                    if op.inc:
                        if inst is None:
                            raise RuntimeError(f"op {op.idx} {op.tag}: dep target lacks instruction")
                        inst.then_inc(sems[op.stream], op.inc)
                if sec == "pool":
                    for d in self.final_deps:
                        dop = self.ops[d]
                        v = self._opval[d]
                        if known[dop.stream] < v:
                            eng.wait_ge(sems[dop.stream], v)
                            known[dop.stream] = v

            @block.tensor
            def _(eng):
                emit_section("pe")

            @block.scalar
            def _(eng):
                emit_section("act")

            @block.vector
            def _(eng):
                emit_section("dve")

            @block.gpsimd
            def _(eng):
                emit_section("pool")

            @block.sync
            def _(eng):
                emit_section("sp")


C_INB = 0
C_DTB = 16
C_CONVW = 32
C_CONVB = 80
C_D = 96
C_OUTB = 112
C_XPB = 120
C_XPBM = 121
C_K = 122
C_ALOG = 126
NCB = C_ALOG + NDC * NST
K_VALS = (EPS, 1.0, -1.0, -0.5)


def build(nc, stages=STAGES):
    s = Sched(nc)
    T = {}

    def dram(name, shape, dt, kind="ExternalInput"):
        T[name] = nc.dram_tensor(name, list(shape), dt, kind=kind).ap()
        return T[name]

    x_in = dram("x", [B, NCC, 128, L], F32)
    out = dram("out", [B, NCC, 128, L], F32, kind="ExternalOutput")
    wi = dram("wi", [D, DI], F32)
    wo = dram("wo", [DI, D], F32)
    wxp = dram("wxp", [DI, RK + NST], F32)
    wdt = dram("wdt", [RK, DI], F32)
    wm = dram("wm", [D, FF], F32)
    wmR = dram("wmR", [FF, D], F32)
    fidx_c = dram("fidx_c", [128, 96], F32)
    idx_sp = {(k, b): dram(f"idx{k}_{b}_sp", [ITER, 1], F32, kind="Internal")
              for k in (2, 3) for b in range(B)}
    wp = dram("wp", [FF, D], F32)
    wc1 = dram("wc1", [D, 256], F32)
    wc2 = dram("wc2", [256, FF], F32)
    cb = dram("cb", [128, NCB], F32)
    ones_c = dram("ones_c", [128, 128], F32)
    onescol_c = dram("onescol_c", [128, 1], F32)
    onescolb_c = dram("onescolb_c", [128, 1], BF16)
    ident_c = dram("ident_c", [128, 128], F32)
    ltri_c = dram("ltri_c", [128, 128], F32)
    bhist_c = dram("bhist_c", [128, 96], F32)
    oneh_c = dram("oneh_c", [16, NST * 128], F32)
    dbk = "ExternalOutput" if DEBUG_SPILL else "Internal"
    wi_bf = dram("wi_bf", [D, DI], BF16, kind="Internal")
    wo_bf = dram("wo_bf", [DI, D], BF16, kind="Internal")
    x1_sp = dram("x1_sp", [B, NCC, 128, L], F32, kind=dbk)
    xin1_sp = dram("xin1_sp", [B, NCC, 128, L], BF16, kind="ExternalInput" if DEBUG_FFN_ONLY else dbk)
    xn_sp = {k: dram(f"xn{k}_sp", [B, NCC, 128, L], BF16, kind=dbk)
             for k in (1, 2, 3)}
    xin3_sp = dram("xin3_sp", [B, NCC, 128, L], BF16, kind=dbk)
    if DEBUG_MAMBA:
        dbg = {
            "dbg_u": dram("dbg_u", [128, NDC * SC], BF16, kind="ExternalOutput"),
            "dbg_dlr": dram("dbg_dlr", [64, SC], BF16, kind="ExternalOutput"),
            "dbg_bm": dram("dbg_bm", [16, SC], BF16, kind="ExternalOutput"),
            "dbg_bmbc": dram("dbg_bmbc", [128, NST * SC], BF16, kind="ExternalOutput"),
            "dbg_delta": dram("dbg_delta", [128, SC], F32, kind="ExternalOutput"),
            "dbg_w": dram("dbg_w", [128, SC], F32, kind="ExternalOutput"),
            "dbg_dA": dram("dbg_dA", [128, SC], F32, kind="ExternalOutput"),
            "dbg_dBu": dram("dbg_dBu", [128, SC], BF16, kind="ExternalOutput"),
            "dbg_S": dram("dbg_S", [128, 8 * SC], BF16, kind="ExternalOutput"),
            "dbg_y": dram("dbg_y", [128, NDC * SC], BF16, kind="ExternalOutput"),
            "dbg_attn": dram("dbg_attn", [128, NCC * SC], BF16, kind="ExternalOutput"),
        }
    if DEBUG_FFN:
        dram("dbg_h", [128, G * TT], BF16, kind="ExternalOutput")
        dram("dbg_xi", [128, NCC * TT], BF16, kind="ExternalOutput")
        dram("dbg_acc", [128, NCC * L], BF16, kind="ExternalOutput")
    cc_in = {k: dram(f"cc_in{k}", [1, 8], F32, kind="Internal") for k in (1, 2)}
    cc_out = {k: dram(f"cc_out{k}", [1, 8], F32, kind="Internal") for k in (1, 2)}

    pb = [nc.alloc_psum_tensor(f"pb{i}", [128, 512], F32).ap() for i in range(8)]

    bufs = {}
    def bf(name):
        if name not in bufs:
            bufs[name] = Buf()
        return bufs[name]
    pbuf = [bf(f"pb{i}") for i in range(8)]

    def sb(name, shape, dt, stack=None):
        if stack is None:
            return nc.alloc_sbuf_tensor(name, list(shape), dt).ap()
        return stack.enter_context(nc.sbuf_tensor(name, list(shape), dt)).ap()

    cb_sb = sb("cb_sb", [128, NCB], F32)
    A_sb = sb("A_sb", [128, NDC * NST], F32)
    ones_sb = sb("ones_sb", [128, 1], F32)
    ones_bf = sb("ones_bf", [128, 1], BF16)
    ones1_sb = sb("ones1_sb", [1, 128], F32)
    ident_sb = sb("ident_sb", [128, 128], F32)
    ltri_sb = sb("ltri_sb", [128, 128], F32)
    bhist_sb = sb("bhist_sb", [128, 96], F32)
    oneh_sb = sb("oneh_sb", [16, NST * 128], BF16)
    ln_sc = sb("ln_sc", [1, 4 * 512], F32)
    sqf_sb = sb("sqf_sb", [128, 512], F32)
    sqb_sb = sb("sqb_sb", [128, 512], BF16)
    fin_sc = sb("fin_sc", [128, 512], F32)
    scal_sb = sb("scal_sb", [1, 16], F32)
    mask_sb = {(k, b): sb(f"mask{k}_{b}", [128, 96], F32)
               for k in (1, 2, 3) for b in range(B)}
    hist2_sb = [sb(f"hist2_{b}", [128, 96], F32) for b in range(B)]
    d_bc = {nm: sb("dbc_" + nm, [128, 1], F32) for nm in ("d2", "d3")}
    pooled_sb = sb("pooled_sb", [128, NCC * 2], F32)

    for i, v in enumerate(K_VALS):
        nc.const_aps.register(F32, float(v), cb_sb[:, C_K + i: C_K + i + 1])
    s.u("sp", "qsp", lambda: nc.sync.dma_start(out=cb_sb, in_=cb), w=[bf("cb")])
    s.u("sp", "qsp", lambda: nc.sync.dma_start(out=ones_sb, in_=onescol_c), w=[bf("ones")])
    s.u("sp", "qsp", lambda: nc.sync.dma_start(out=ones1_sb, in_=ones_c[0:1, :]), w=[bf("ones1")])
    s.u("sp", "qsp", lambda: nc.sync.dma_start(out=ident_sb, in_=ident_c), w=[bf("ident")])
    s.u("sp", "qsp", lambda: nc.sync.dma_start(out=ltri_sb, in_=ltri_c), w=[bf("ltri")])
    s.u("sp", "qsp", lambda: nc.sync.dma_start(out=bhist_sb, in_=bhist_c), w=[bf("bhist")])
    s.u("pool", "qgp", lambda: nc.gpsimd.dma_start(out=oneh_sb, in_=oneh_c), w=[bf("oneh")])
    s.u("sp", "qsp", lambda: nc.sync.dma_start(out=ones_bf, in_=onescolb_c), w=[bf("onesbf")])
    s.u("act", "act", lambda: nc.scalar.activation(A_sb, cb_sb[:, C_ALOG:C_ALOG + NDC * NST], AF.Exp),
        r=[bf("cb")], w=[bf("A")], tag="Aexp")
    s.u("dve", "dve", lambda: nc.vector.tensor_scalar_mul(A_sb, A_sb, -1.0), w=[bf("A")], tag="Aneg")
    s.u("dve", "dve", lambda: nc.vector.memset(scal_sb, 0.0), w=[bf("scal")])

    def cbcol(off, idx, n=1):
        return cb_sb[:, off + idx: off + idx + n]

    ln_ctr = [0]

    def ln(srcs, Tn, tag):
        fp32src = srcs[0][0].tensor.dtype == F32
        one = ones_sb if fp32src else ones_bf
        onebuf = bf("ones") if fp32src else bf("onesbf")
        sq = sqf_sb if fp32src else sqb_sb
        sqB = bf("sqf" if fp32src else "sqb")
        ps_sum = pb[6][:1, :Tn]
        ps_sq = pb[7][:1, :Tn]
        nsrc = len(srcs)
        for i, (ap, ab) in enumerate(srcs):
            s.u("dve", "dve", lambda ap=ap: nc.vector.tensor_mul(sq[:, :Tn], ap, ap),
                r=[ab], w=[sqB], tag=tag + "sq")
            s.u("pe", "pe", lambda ap=ap, i=i: nc.tensor.matmul(
                ps_sum, one, ap, start=(i == 0), stop=(i == nsrc - 1)),
                r=[ab, onebuf], w=[pbuf[6]], tag=tag + "msum")
            s.u("pe", "pe", lambda i=i: nc.tensor.matmul(
                ps_sq, one, sq[:, :Tn], start=(i == 0), stop=(i == nsrc - 1)),
                r=[sqB, onebuf], w=[pbuf[7]], tag=tag + "msq")
        kk = ln_ctr[0] % 2
        ln_ctr[0] += 1
        mrow = ln_sc[:, kk * 1024: kk * 1024 + Tn]
        vrow = ln_sc[:, kk * 1024 + 512: kk * 1024 + 512 + Tn]
        rowb = bf(f"lnrow{kk}")
        s.u("dve", "dve", lambda: nc.vector.tensor_scalar_mul(mrow, ps_sum, 1.0 / D),
            r=[pbuf[6]], w=[rowb], tag=tag + "mean")
        s.u("dve", "dve", lambda: nc.vector.tensor_mul(vrow, mrow, mrow),
            r=[rowb], w=[rowb], tag=tag + "m2")
        s.u("dve", "dve", lambda: nc.vector.scalar_tensor_tensor(
            vrow, ps_sq, 1.0 / D, vrow, OP.mult, OP.subtract),
            r=[pbuf[7], rowb], w=[rowb], tag=tag + "var")
        s.u("act", "act", lambda: nc.scalar.activation(vrow, vrow, AF.Ln, bias=EPS),
            r=[rowb, bf("cb")], w=[rowb], tag=tag + "lnv")
        s.u("act", "act", lambda: nc.scalar.activation(vrow, vrow, AF.Exp, scale=-0.5),
            r=[rowb, bf("cb")], w=[rowb], tag=tag + "rstd")
        mbc = pb[6][:, :Tn]
        rbc = pb[7][:, :Tn]
        s.u("pe", "pe", lambda: nc.tensor.matmul(mbc, ones1_sb, mrow, start=True, stop=True),
            r=[rowb, bf("ones1")], w=[pbuf[6]], tag=tag + "mbc")
        s.u("pe", "pe", lambda: nc.tensor.matmul(rbc, ones1_sb, vrow, start=True, stop=True),
            r=[rowb, bf("ones1")], w=[pbuf[7]], tag=tag + "rbc")
        return mbc, rbc, pbuf[6], pbuf[7]

    ph1 = contextlib.ExitStack()
    xld = [sb(f"xld{i}", [128, NCC * TC], BF16, ph1) for i in range(2)]
    ep_x = sb("ep_x", [128, NCC * SC], F32, ph1)
    hpad = [sb(f"hpad{i}", [128, NDC * (TC + 2)], BF16, ph1) for i in range(2)]
    lcol2 = [sb(f"lcol2_{b}", [128, NDC * 2], BF16, ph1) for b in range(B)]
    wi_rb = [sb(f"wi_rb{i}", [128, 8 * 256], BF16, ph1) for i in range(2)]
    wo_rb = [sb(f"wo_rb{i}", [128, NDC * 128], BF16, ph1) for i in range(2)]
    wxp_sb = sb("wxp_sb", [128, NDC * (RK + NST)], BF16, ph1)
    wdt_sb = sb("wdt_sb", [64, DI], BF16, ph1)
    u_sc = [sb(f"u_sc{i}", [128, NDC * SC], BF16, ph1) for i in range(2)]
    dlr_bf = [sb(f"dlr_bf{i}", [64, SC], BF16, ph1) for i in range(2)]
    bm_sb = [sb(f"bm_sb{i}", [16, SC], BF16, ph1) for i in range(2)]
    bmbc = sb("bmbc", [128, NST * SC], BF16, ph1)
    delta_t = [sb(f"delta{i}", [128, SC], F32, ph1) for i in range(3)]
    w_t = [sb(f"w_t{i}", [128, SC], F32, ph1) for i in range(3)]
    cv_t = [sb(f"cv_t{i}", [128, TC], F32, ph1) for i in range(2)]
    dA_t = [sb(f"dA{i}", [128, SC], F32, ph1) for i in range(3)]
    dBu_t = [sb(f"dBu{i}", [128, SC], BF16, ph1) for i in range(3)]
    S_h = sb("S_h", [128, 8 * SC], BF16, ph1)
    rd1 = sb("rd1", [128, SC], F32, ph1)
    rd2 = sb("rd2", [128, SC], F32, ph1)
    yred2 = sb("yred2", [128, SC], F32, ph1)
    ytr = sb("ytr", [128, 4 * SC], BF16, ph1)
    y_sb = sb("y_sb", [128, NDC * SC], BF16, ph1)
    attn_sb = sb("attn_sb", [128, NCC * SC], BF16, ph1)
    state_sb = [sb(f"state{b}", [128, NDC * NST], F32, ph1) for b in range(B)]

    s.u("pool", "qgp", lambda: nc.gpsimd.dma_start(
        out=wxp_sb.rearrange("p (dk r) -> p dk r", dk=NDC),
        in_=wxp.rearrange("(dk p) r -> p dk r", p=128)), w=[bf("wxp")])
    s.u("pool", "qgp", lambda: nc.gpsimd.dma_start(out=wdt_sb, in_=wdt), w=[bf("wdt")])
    pc_f = [ep_x[:, i * 2048:(i + 1) * 2048] for i in range(2)]
    pc_b = [attn_sb[:, i * 2048:(i + 1) * 2048] for i in range(2)]
    pcj = [0]

    def precast(src_t, dst_t, rows, cols):
        for r0 in range(0, rows, 128):
            j = pcj[0] % 2
            pcj[0] += 1
            fB, bB = bf("ep_x"), bf("attn")
            s.u("sp", "qsp", lambda r0=r0, j=j: nc.sync.dma_start(
                out=pc_f[j][:, :cols], in_=src_t[r0:r0 + 128, :]), w=[fB],
                tag=f"pcl{r0}")
            s.u("act", "act", lambda j=j: nc.scalar.activation(
                pc_b[j][:, :cols], pc_f[j][:, :cols], AF.Copy), r=[fB], w=[bB],
                tag=f"pcc{r0}")
            s.u("sp", "qsp", lambda r0=r0, j=j: nc.sync.dma_start(
                out=dst_t[r0:r0 + 128, :], in_=pc_b[j][:, :cols]), r=[bB],
                w=[bf("wibf" if dst_t is wi_bf else "wobf")], tag=f"pcs{r0}")

    precast(wi, wi_bf, D, DI)
    precast(wo, wo_bf, DI, D)

    def front(b, tc):
        xl = xld[tc % 2]
        xlB = bf(f"xld{tc % 2}")
        t0 = tc * TC
        s.u("pool", "qgp", lambda xl=xl: nc.gpsimd.dma_start(
            out=xl.rearrange("p (cc t) -> p cc t", cc=NCC),
            in_=x_in[b, :, :, t0:t0 + TC].rearrange("cc p t -> p cc t")), w=[xlB], tag=f"ldx{b}_{tc}")
        srcs = [(xl[:, cc * TC:(cc + 1) * TC], xlB) for cc in range(NCC)]
        mbc, rbc, mB, rB = ln(srcs, TC, f"ln1_{b}_{tc}")
        for cc in range(NCC):
            ap = srcs[cc][0]
            s.u("dve", "dve", lambda ap=ap: nc.vector.tensor_sub(
                fin_sc[:, :TC], ap, mbc), r=[xlB, mB], w=[bf("fin")],
                tag=f"ln1a{b}_{tc}_{cc}")
            s.u("dve", "dve", lambda ap=ap: nc.vector.tensor_mul(
                ap, fin_sc[:, :TC], rbc), r=[bf("fin"), rB], w=[xlB],
                tag=f"ln1b{b}_{tc}_{cc}")
        hp = hpad[tc % 2]
        hpB = bf(f"hpad{tc % 2}")
        for gq in range(8):
            wr = wi_rb[gq % 2]
            wrB = bf(f"wi_rb{gq % 2}")
            s.u("sp", "qsp", lambda gq=gq, wr=wr: nc.sync.dma_start(
                out=wr.rearrange("p (kc dm) -> p kc dm", kc=8),
                in_=wi_bf[:, gq * 256:(gq + 1) * 256].rearrange("(kc p) dm -> p kc dm", p=128)),
                r=[bf("wibf")], w=[wrB], tag=f"ldwi{gq}_{b}_{tc}")
            for dcl in range(2):
                dc = gq * 2 + dcl
                bank = dc % 2
                ph = pb[bank][:, :TC]
                for kc in range(8):
                    s.u("pe", "pe", lambda ph=ph, wr=wr, kc=kc, dcl=dcl, xl=xl: nc.tensor.matmul(
                        ph, wr[:, kc * 256 + dcl * 128: kc * 256 + (dcl + 1) * 128],
                        xl[:, kc * TC:(kc + 1) * TC], start=(kc == 0), stop=(kc == 7)),
                        r=[wrB, xlB], w=[pbuf[bank]], tag=f"inproj{b}_{tc}_{dc}")
                s.u("dve", "dve", lambda ph=ph, dc=dc, hp=hp: nc.vector.tensor_scalar(
                    hp[:, dc * (TC + 2) + 2: dc * (TC + 2) + 2 + TC], ph,
                    cbcol(C_INB, dc), None, OP.add), r=[pbuf[bank], bf("cb")], w=[hpB],
                    tag=f"hevac{b}_{tc}_{dc}")
        hp3 = hp.rearrange("p (dc t) -> p dc t", dc=NDC)
        if tc > 0:
            lc3 = lcol2[b].rearrange("p (dc t) -> p dc t", dc=NDC)
            s.u("dve", "dve", lambda hp3=hp3, lc3=lc3: nc.vector.tensor_copy(
                hp3[:, :, 0:2], lc3), r=[bf(f"lcol{b}")], w=[hpB], tag=f"halo{b}_{tc}")
        else:
            s.u("dve", "dve", lambda hp3=hp3: nc.vector.memset(hp3[:, :, 0:2], 0.0),
                w=[hpB], tag=f"halo0{b}_{tc}")

    def conv_shifted(b, tc):
        hp = hpad[tc % 2]
        hpB = bf(f"hpad{tc % 2}")
        g = tc // 2
        j0 = 1 if tc == 0 else 0
        for dc in range(NDC):
            base = dc * (TC + 2)
            cv = cv_t[dc % 2]
            cvB = bf(f"cv_t{dc % 2}")
            nconv = TC - j0
            s.u("dve", "dve", lambda cv=cv, base=base, dc=dc, j0=j0, nconv=nconv, hp=hp:
                nc.vector.tensor_scalar(
                    cv[:, :nconv], hp[:, base + j0 + 1: base + j0 + 1 + nconv],
                    cbcol(C_CONVW, dc * 3 + 1), None, OP.mult),
                r=[hpB, bf("cb")], w=[cvB], tag=f"cv1_{b}_{tc}_{dc}")
            s.u("dve", "dve", lambda cv=cv, base=base, dc=dc, j0=j0, nconv=nconv, hp=hp:
                nc.vector.scalar_tensor_tensor(
                    cv[:, :nconv], hp[:, base + j0: base + j0 + nconv],
                    cbcol(C_CONVW, dc * 3 + 0), cv[:, :nconv], OP.mult, OP.add),
                r=[hpB, bf("cb")], w=[cvB], tag=f"cv0_{b}_{tc}_{dc}")
            s.u("dve", "dve", lambda cv=cv, base=base, dc=dc, j0=j0, nconv=nconv, hp=hp:
                nc.vector.scalar_tensor_tensor(
                    cv[:, :nconv], hp[:, base + j0 + 2: base + j0 + 2 + nconv],
                    cbcol(C_CONVW, dc * 3 + 2), cv[:, :nconv], OP.mult, OP.add),
                r=[hpB, bf("cb")], w=[cvB], tag=f"cv2_{b}_{tc}_{dc}")
            if tc == 0:
                usc = u_sc[0]
                uB = bf("u0")
                s.u("act", "act", lambda cv=cv, dc=dc, usc=usc: nc.scalar.activation(
                    usc[:, dc * SC: dc * SC + TC - 1], cv[:, :TC - 1], AF.Silu,
                    bias=cbcol(C_CONVB, dc)), r=[cvB, bf("cb")], w=[uB],
                    tag=f"silu{b}_{tc}_{dc}")
            elif tc % 2 == 0:
                up = u_sc[(g - 1) % 2]
                upB = bf(f"u{(g - 1) % 2}")
                uc = u_sc[g % 2]
                ucB = bf(f"u{g % 2}")
                s.u("act", "act", lambda cv=cv, dc=dc, up=up: nc.scalar.activation(
                    up[:, dc * SC + SC - 1: dc * SC + SC], cv[:, 0:1], AF.Silu,
                    bias=cbcol(C_CONVB, dc)), r=[cvB, bf("cb")], w=[upB],
                    tag=f"siluA{b}_{tc}_{dc}")
                s.u("act", "act", lambda cv=cv, dc=dc, uc=uc: nc.scalar.activation(
                    uc[:, dc * SC: dc * SC + TC - 1], cv[:, 1:TC], AF.Silu,
                    bias=cbcol(C_CONVB, dc)), r=[cvB, bf("cb")], w=[ucB],
                    tag=f"siluB{b}_{tc}_{dc}")
            else:
                uc = u_sc[g % 2]
                ucB = bf(f"u{g % 2}")
                s.u("act", "act", lambda cv=cv, dc=dc, uc=uc: nc.scalar.activation(
                    uc[:, dc * SC + TC - 1: dc * SC + 2 * TC - 1], cv, AF.Silu,
                    bias=cbcol(C_CONVB, dc)), r=[cvB, bf("cb")], w=[ucB],
                    tag=f"silu{b}_{tc}_{dc}")
            if tc == NTC - 1:
                uc = u_sc[(NSC - 1) % 2]
                ucB = bf(f"u{(NSC - 1) % 2}")
                s.u("dve", "dve", lambda cv=cv, base=base, dc=dc, hp=hp:
                    nc.vector.tensor_scalar(
                        cv[:, 0:1], hp[:, base + TC + 1: base + TC + 2],
                        cbcol(C_CONVW, dc * 3 + 1), None, OP.mult),
                    r=[hpB, bf("cb")], w=[cvB], tag=f"cvE1_{b}_{dc}")
                s.u("dve", "dve", lambda cv=cv, base=base, dc=dc, hp=hp:
                    nc.vector.scalar_tensor_tensor(
                        cv[:, 0:1], hp[:, base + TC: base + TC + 1],
                        cbcol(C_CONVW, dc * 3 + 0), cv[:, 0:1], OP.mult, OP.add),
                    r=[hpB, bf("cb")], w=[cvB], tag=f"cvE0_{b}_{dc}")
                s.u("act", "act", lambda cv=cv, dc=dc, uc=uc: nc.scalar.activation(
                    uc[:, dc * SC + SC - 1: dc * SC + SC], cv[:, 0:1], AF.Silu,
                    bias=cbcol(C_CONVB, dc)), r=[cvB, bf("cb")], w=[ucB],
                    tag=f"siluE{b}_{dc}")
        hp3 = hp.rearrange("p (dc t) -> p dc t", dc=NDC)
        lc3 = lcol2[b].rearrange("p (dc t) -> p dc t", dc=NDC)
        s.u("dve", "dve", lambda hp3=hp3, lc3=lc3: nc.vector.tensor_copy(
            lc3, hp3[:, :, TC:TC + 2]), r=[hpB], w=[bf(f"lcol{b}")],
            tag=f"lcap{b}_{tc}")

    def scan_group(b, g):
        usc = u_sc[g % 2]
        uB = bf(f"u{g % 2}")
        if g == 0:
            s.u("dve", "dve", lambda: nc.vector.memset(state_sb[b], 0.0),
                w=[bf(f"st{b}_{dcx}") for dcx in range(NDC)], tag=f"stinit{b}")
        pdl = pb[2][:64, :SC]
        pbm = pb[3][:16, :SC]
        for dc in range(NDC):
            s.u("pe", "pe", lambda dc=dc, usc=usc: nc.tensor.matmul(
                pdl, wxp_sb[:, dc * 80: dc * 80 + 64], usc[:, dc * SC:(dc + 1) * SC],
                start=(dc == 0), stop=(dc == NDC - 1)),
                r=[bf("wxp"), uB], w=[pbuf[2]], tag=f"xpd{b}_{g}_{dc}")
            s.u("pe", "pe", lambda dc=dc, usc=usc: nc.tensor.matmul(
                pbm, wxp_sb[:, dc * 80 + 64: dc * 80 + 80], usc[:, dc * SC:(dc + 1) * SC],
                start=(dc == 0), stop=(dc == NDC - 1)),
                r=[bf("wxp"), uB], w=[pbuf[3]], tag=f"xpb{b}_{g}_{dc}")
        dlb = dlr_bf[g % 2]
        dlbB = bf(f"dlr{g % 2}")
        s.u("dve", "dve", lambda dlb=dlb: nc.vector.tensor_scalar(
            dlb, pdl, cb_sb[:64, C_XPB:C_XPB + 1], None, OP.add),
            r=[pbuf[2], bf("cb")], w=[dlbB], tag=f"dlrevac{b}_{g}")
        bm = bm_sb[g % 2]
        bmB = bf(f"bm{g % 2}")
        s.u("dve", "dve", lambda bm=bm: nc.vector.tensor_scalar(
            bm, pbm, cb_sb[:16, C_XPBM:C_XPBM + 1], None, OP.add),
            r=[pbuf[3], bf("cb")], w=[bmB], tag=f"bmevac0{b}_{g}")
        for n in range(NST):
            bank = 2 + (n % 2)
            pn = pb[bank][:, :SC]
            s.u("pe", "pe", lambda pn=pn, n=n, bm=bm: nc.tensor.matmul(
                pn, oneh_sb[:, n * 128:(n + 1) * 128], bm, start=True, stop=True),
                r=[bmB, bf("oneh")], w=[pbuf[bank]], tag=f"bmbc{b}_{g}_{n}")
            s.u("act", "act", lambda pn=pn, n=n: nc.scalar.activation(
                bmbc[:, n * SC:(n + 1) * SC], pn, AF.Copy),
                r=[pbuf[bank]], w=[bf("bmbc")], tag=f"bmevac{b}_{g}_{n}")
        for dc in range(NDC):
            bank = 2 + (dc % 2)
            pdt = pb[bank][:, :SC]
            s.u("pe", "pe", lambda pdt=pdt, dc=dc, dlb=dlb: nc.tensor.matmul(
                pdt, wdt_sb[:, dc * 128:(dc + 1) * 128], dlb, start=True, stop=True),
                r=[bf("wdt"), dlbB], w=[pbuf[bank]], tag=f"dtmm{b}_{g}_{dc}")
            dl = delta_t[dc % 3]
            dlB = bf(f"delta{dc % 3}")
            sp_ = w_t[dc % 3]
            spB = bf(f"w_t{dc % 3}")
            s.u("dve", "dve", lambda pdt=pdt, dl=dl, dc=dc: nc.vector.tensor_scalar(
                dl, pdt, cbcol(C_DTB, dc), None, OP.add), r=[pbuf[bank], bf("cb")],
                w=[dlB], tag=f"spbias{b}_{g}_{dc}")
            s.u("dve", "dve", lambda dl=dl, sp_=sp_: nc.vector.scalar_tensor_tensor(
                sp_, dl, -1.0, dl, OP.mult, OP.max), r=[dlB], w=[spB],
                tag=f"spabs{b}_{g}_{dc}")
            s.u("act", "act", lambda sp_=sp_: nc.scalar.activation(sp_, sp_, AF.Exp, scale=-1.0),
                r=[spB, bf("cb")], w=[spB], tag=f"spexp{b}_{g}_{dc}")
            s.u("act", "act", lambda sp_=sp_: nc.scalar.activation(sp_, sp_, AF.Ln, bias=1.0),
                r=[spB, bf("cb")], w=[spB], tag=f"spln{b}_{g}_{dc}")
            s.u("dve", "dve", lambda sp_=sp_, dl=dl: nc.vector.scalar_tensor_tensor(
                dl, dl, 0.0, sp_, OP.max, OP.add), r=[spB], w=[dlB],
                tag=f"spfin{b}_{g}_{dc}")
            wt = w_t[dc % 3]
            wtB = bf(f"w_t{dc % 3}")
            s.u("pool", "pool", lambda dl=dl, wt=wt, dc=dc, usc=usc: nc.gpsimd.tensor_mul(
                wt, dl, usc[:, dc * SC:(dc + 1) * SC]), r=[dlB, uB], w=[wtB],
                tag=f"wmul{b}_{g}_{dc}")
            for half in range(2):
                for nn in range(8):
                    n = half * 8 + nn
                    da = dA_t[n % 3]
                    daB = bf(f"dA{n % 3}")
                    db_ = dBu_t[n % 3]
                    dbB = bf(f"dBu{n % 3}")
                    s.u("act", "act", lambda dl=dl, da=da, dc=dc, n=n: nc.scalar.activation(
                        da, dl, AF.Exp, scale=A_sb[:, dc * NST + n: dc * NST + n + 1]),
                        r=[dlB, bf("A")], w=[daB], tag=f"dA{b}_{g}_{dc}_{n}")
                    if n % 2 == 0:
                        s.u("dve", "dve", lambda wt=wt, db_=db_, n=n: nc.vector.tensor_mul(
                            db_, wt, bmbc[:, n * SC:(n + 1) * SC]), r=[wtB, bf("bmbc")],
                            w=[dbB], tag=f"dBu{b}_{g}_{dc}_{n}")
                    else:
                        s.u("pool", "pool", lambda wt=wt, db_=db_, n=n: nc.gpsimd.tensor_mul(
                            db_, wt, bmbc[:, n * SC:(n + 1) * SC]), r=[wtB, bf("bmbc")],
                            w=[dbB], tag=f"dBu{b}_{g}_{dc}_{n}")
                    s.u("dve", "dve", lambda da=da, db_=db_, dc=dc, n=n, nn=nn:
                        nc.vector.tensor_tensor_scan(
                            S_h[:, nn * SC:(nn + 1) * SC], da, db_,
                            state_sb[b][:, dc * NST + n: dc * NST + n + 1],
                            OP.mult, OP.add), r=[daB, dbB, bf(f"st{b}_{dc}")], w=[bf("S")],
                        tag=f"scan{b}_{g}_{dc}_{n}")
                s.u("pool", "pool", lambda dc=dc, half=half: nc.gpsimd.tensor_copy(
                    state_sb[b][:, dc * NST + half * 8: dc * NST + half * 8 + 8],
                    S_h.rearrange("p (n t) -> p n t", n=8)[:, :, SC - 1:SC]),
                    r=[bf("S")], w=[bf(f"st{b}_{dc}")], tag=f"stsave{b}_{g}_{dc}_{half}")
                rd = rd1 if half == 0 else rd2
                rdB = bf("rd1" if half == 0 else "rd2")
                for pr in range(4):
                    s.u("pool", "pool", lambda pr=pr: nc.gpsimd.tensor_add(
                        ytr[:, pr * SC:(pr + 1) * SC],
                        S_h[:, (2 * pr) * SC:(2 * pr + 1) * SC],
                        S_h[:, (2 * pr + 1) * SC:(2 * pr + 2) * SC]),
                        r=[bf("S")], w=[bf("ytr")], tag=f"yt{b}_{g}_{dc}_{half}_{pr}")
                s.u("dve", "dve", lambda rd=rd: nc.vector.tensor_add(
                    rd, ytr[:, 0:SC], ytr[:, SC:2 * SC]), r=[bf("ytr")], w=[rdB],
                    tag=f"yta{b}_{g}_{dc}_{half}")
                s.u("dve", "dve", lambda: nc.vector.tensor_add(
                    yred2, ytr[:, 2 * SC:3 * SC], ytr[:, 3 * SC:4 * SC]),
                    r=[bf("ytr")], w=[bf("yred2")], tag=f"ytb{b}_{g}_{dc}_{half}")
                s.u("dve", "dve", lambda rd=rd: nc.vector.tensor_add(
                    rd, rd, yred2), r=[bf("yred2")], w=[rdB],
                    tag=f"ytc{b}_{g}_{dc}_{half}")
            s.u("dve", "dve", lambda: nc.vector.tensor_add(rd1, rd1, rd2),
                r=[bf("rd2")], w=[bf("rd1")], tag=f"yadd{b}_{g}_{dc}")
            s.u("dve", "dve", lambda dc=dc, usc=usc: nc.vector.scalar_tensor_tensor(
                y_sb[:, dc * SC:(dc + 1) * SC], usc[:, dc * SC:(dc + 1) * SC],
                cbcol(C_D, dc), rd1, OP.mult, OP.add), r=[bf("rd1"), uB, bf("cb")],
                w=[bf("y")], tag=f"y2_{b}_{g}_{dc}")
            if DEBUG_MAMBA and b == 0 and g == 0 and dc == 0:
                for nm, ap_, rb_ in (("dbg_delta", dl, bf("delta0")),
                                     ("dbg_w", wt, bf("w_t0")),
                                     ("dbg_dA", dA_t[1], bf("dA1")),
                                     ("dbg_dBu", dBu_t[1], bf("dBu1")),
                                     ("dbg_S", S_h, bf("S"))):
                    s.u("sp", "qsp", lambda nm=nm, ap_=ap_: nc.sync.dma_start(
                        out=T[nm], in_=ap_), r=[rb_], tag="dump" + nm)

    def epilogue(b, g):
        if DEBUG_MAMBA and b == 0 and g == 0:
            for nm, ap_, rb_ in (("dbg_u", u_sc[0], bf("u0")),
                                 ("dbg_dlr", dlr_bf[0], bf("dlr0")),
                                 ("dbg_bm", bm_sb[0], bf("bm0")),
                                 ("dbg_bmbc", bmbc, bf("bmbc")),
                                 ("dbg_y", y_sb, bf("y"))):
                s.u("sp", "qsp", lambda nm=nm, ap_=ap_: nc.sync.dma_start(
                    out=T[nm], in_=ap_), r=[rb_], tag="dump" + nm)
        exB = bf("ep_x")
        s.u("sp", "qsp", lambda b=b, g=g: nc.sync.dma_start(
            out=ep_x.rearrange("p (cc t) -> p cc t", cc=NCC),
            in_=x_in[b, :, :, g * SC:(g + 1) * SC].rearrange("cc p t -> p cc t")), w=[exB], tag=f"ldepx{b}_{g}")
        for m in range(NCC):
            wr = wo_rb[m % 2]
            wrB = bf(f"wo_rb{m % 2}")
            s.u("sp", "qsp", lambda m=m, wr=wr: nc.sync.dma_start(
                out=wr.rearrange("p (dk mm) -> p dk mm", dk=NDC),
                in_=wo_bf[:, m * 128:(m + 1) * 128].rearrange("(dk p) mm -> p dk mm", p=128)),
                r=[bf("wobf")], w=[wrB], tag=f"ldwo{m}_{b}_{g}")
            bank = 4 + (m % 2)
            po = pb[bank][:, :SC]
            for dk in range(NDC):
                s.u("pe", "pe", lambda po=po, wr=wr, dk=dk: nc.tensor.matmul(
                    po, wr[:, dk * 128:(dk + 1) * 128], y_sb[:, dk * SC:(dk + 1) * SC],
                    start=(dk == 0), stop=(dk == NDC - 1)),
                    r=[wrB, bf("y")], w=[pbuf[bank]], tag=f"outproj{b}_{g}_{m}")
            s.u("dve", "dve", lambda po=po, m=m: nc.vector.tensor_scalar(
                attn_sb[:, m * SC:(m + 1) * SC], po, cbcol(C_OUTB, m), None, OP.add),
                r=[pbuf[bank], bf("cb")], w=[bf("attn")], tag=f"attnevac{b}_{g}_{m}")
        if DEBUG_MAMBA and b == 0 and g == 0:
            s.u("sp", "qsp", lambda: nc.sync.dma_start(
                out=T["dbg_attn"], in_=attn_sb), r=[bf("attn")], tag="dumpattn")
        srcs = [(attn_sb[:, cc * SC:(cc + 1) * SC], bf("attn")) for cc in range(NCC)]
        mbc, rbc, mB, rB = ln(srcs, SC, f"ln2_{b}_{g}")
        for cc in range(NCC):
            ap = srcs[cc][0]
            xcc = ep_x[:, cc * SC:(cc + 1) * SC]
            s.u("dve", "dve", lambda ap=ap: nc.vector.tensor_sub(
                fin_sc, ap, mbc), r=[bf("attn"), mB], w=[bf("fin")],
                tag=f"ln2a{b}_{g}_{cc}")
            s.u("dve", "dve", lambda: nc.vector.tensor_mul(
                fin_sc, fin_sc, rbc), r=[rB], w=[bf("fin")],
                tag=f"ln2b{b}_{g}_{cc}")
            s.u("dve", "dve", lambda xcc=xcc: nc.vector.tensor_add(
                xcc, xcc, fin_sc), r=[bf("fin")], w=[exB],
                tag=f"ln2c{b}_{g}_{cc}")
        s.u("sp", "qsp", lambda b=b, g=g: nc.sync.dma_start(
            out=x1_sp[b, :, :, g * SC:(g + 1) * SC].rearrange("cc p t -> p cc t"),
            in_=ep_x.rearrange("p (cc t) -> p cc t", cc=NCC)), r=[exB],
            w=[bf("x1_sp")], tag=f"stx1_{b}_{g}")
        srcs = [(ep_x[:, cc * SC:(cc + 1) * SC], exB) for cc in range(NCC)]
        mbc, rbc, mB, rB = ln(srcs, SC, f"ln3_{b}_{g}")
        for cc in range(NCC):
            ap = srcs[cc][0]
            dst = attn_sb[:, cc * SC:(cc + 1) * SC]
            s.u("dve", "dve", lambda ap=ap: nc.vector.tensor_sub(
                fin_sc, ap, mbc), r=[exB, mB], w=[bf("fin")],
                tag=f"ln3a{b}_{g}_{cc}")
            s.u("dve", "dve", lambda dst=dst: nc.vector.tensor_mul(
                dst, fin_sc, rbc), r=[bf("fin"), rB], w=[bf("attn")],
                tag=f"ln3b{b}_{g}_{cc}")
        s.u("sp", "qsp", lambda b=b, g=g: nc.sync.dma_start(
            out=xin1_sp[b, :, :, g * SC:(g + 1) * SC].rearrange("cc p t -> p cc t"),
            in_=attn_sb.rearrange("p (cc t) -> p cc t", cc=NCC)), r=[bf("attn")],
            w=[bf("xin1_sp")], tag=f"stxin1_{b}_{g}")

    def mamba():
        for b in range(B):
            for tc in range(NTC):
                front(b, tc)
                conv_shifted(b, tc)
                if tc >= 2 and tc % 2 == 0:
                    scan_group(b, tc // 2 - 1)
                    epilogue(b, tc // 2 - 1)
            scan_group(b, NSC - 1)
            epilogue(b, NSC - 1)

    if "mamba" in stages:
        mamba()
    ph1.close()
    s.fence()

    ph2 = contextlib.ExitStack()
    acc = sb("acc", [128, NCC * L], BF16, ph2)
    wm_rb = [sb(f"wm_rb{i}", [128, 8 * G * 128], BF16, ph2) for i in range(2)]
    wp_rb = [sb(f"wp_rb{i}", [128, G * 1024], BF16, ph2) for i in range(2)]
    xi_rb = [sb(f"xi_rb{i}", [128, NCC * TT], BF16, ph2) for i in range(3)]
    hbuf = sb("hbuf", [128, G * TT], BF16, ph2)
    wc1_sb = sb("wc1_sb", [128, 8 * 256], F32, ph2)
    wc2_rb = [sb(f"wc2_rb{i}", [128, 2 * 8 * 128], BF16, ph2) for i in range(2)]
    y1_bf = sb("y1_bf", [128, 4], BF16, ph2)
    ck_sb = sb("ck_sb", [128, 2 * 96], F32, ph2)
    gt_scr = sb("gt_scr", [128, 96], F32, ph2)
    gt_scr2 = sb("gt_scr2", [128, 96], F32, ph2)
    rank_sb = sb("rank_sb", [128, 96], F32, ph2)
    cbc_sb = sb("cbc_sb", [128, 96], F32, ph2)
    tpad_sb = sb("tpad_sb", [128, 128], F32, ph2)
    colrow = sb("colrow", [1, 4 * 96], F32, ph2)
    bis = {nm: [sb(f"bis_{nm}{b}", [128, 1],
                   mybir.dt.uint8 if nm == "sel" else F32, ph2) for b in range(B)]
           for nm in ("lo", "hi", "th", "cnt", "sel", "r")}
    fidx_sb = sb("fidx_sb", [128, 96], F32, ph2)
    ranki_sb = sb("ranki_sb", [128, 96], mybir.dt.int32, ph2)
    idxi_sb = {(k, b): sb(f"idxi{k}_{b}", [128, 32], mybir.dt.int32, ph2)
               for k in (2, 3) for b in range(B)}
    idxf_sb = sb("idxf_sb", [128, 32], F32, ph2)
    wmg_f32 = [sb(f"wmg{i}", [128, 1024], F32, ph2) for i in range(4)]
    wpg_f32 = [sb(f"wpg{i}", [128, 1024], F32, ph2) for i in range(4)]
    s.u("sp", "qsp", lambda: nc.sync.dma_start(out=fidx_sb, in_=fidx_c), w=[bf("fidx")])

    s.u("sp", "qsp", lambda: nc.sync.dma_start(
        out=wc1_sb.rearrange("p (kc m) -> p kc m", kc=8),
        in_=wc1.rearrange("(kc p) m -> p kc m", p=128)), w=[bf("wc1")])

    xi_ctr = [0]
    ind_toks = []
    bc_reg = nc.gpsimd.alloc_register("bcreg")
    s.add("pool", "pool", lambda: nc.gpsimd.reg_mov(bc_reg, ITER - 1), tag="bcreg")

    def ind_dep():
        return [ind_toks[-24]] if len(ind_toks) >= 24 else []

    def build_indices(k, b):
        mk = mask_sb[(1 if k == 2 else 3, b)]
        mkB = bf(f"mask{1 if k == 2 else 3}_{b}")
        p96 = pb[4][:1, :96]
        s.u("pe", "pe", lambda: nc.tensor.matmul(
            p96, ones_sb, mk, start=True, stop=True),
            r=[bf("ones"), mkB], w=[pbuf[4]], tag=f"gcol{k}_{b}")
        s.u("dve", "dve", lambda: nc.vector.tensor_tensor_scan(
            colrow[:, 288:384], ones1_sb[:1, :96], p96, 0.0, OP.mult, OP.add),
            r=[pbuf[4], bf("ones1")], w=[bf("crow")], tag=f"gcum{k}_{b}")
        s.u("dve", "dve", lambda: nc.vector.tensor_sub(
            colrow[:, 288:384], colrow[:, 288:384], p96), r=[pbuf[4]],
            w=[bf("crow")], tag=f"gex{k}_{b}")
        pc96 = pb[4][:, :96]
        s.u("pe", "pe", lambda: nc.tensor.matmul(
            pc96, ones1_sb, colrow[:1, 288:384], start=True, stop=True),
            r=[bf("crow"), bf("ones1")], w=[pbuf[4]], tag=f"gcbc{k}_{b}")
        s.u("dve", "dve", lambda: nc.vector.tensor_copy(cbc_sb, pc96),
            r=[pbuf[4]], w=[bf("cbc")], tag=f"gcbcc{k}_{b}")
        pp = pb[5][:, :96]
        s.u("pe", "pe", lambda: nc.tensor.matmul(
            pp, ltri_sb, mk, start=True, stop=True),
            r=[bf("ltri"), mkB], w=[pbuf[5]], tag=f"gpfx{k}_{b}")
        s.u("dve", "dve", lambda: nc.vector.tensor_add(rank_sb, cbc_sb, pp),
            r=[bf("cbc"), pbuf[5]], w=[bf("rank")], tag=f"grank{k}_{b}")
        s.u("dve", "dve", lambda: nc.vector.tensor_scalar(
            gt_scr, mk, -2.0e9, 2.0e9, OP.mult, OP.add), r=[mkB], w=[bf("gts")],
            tag=f"gskip{k}_{b}")
        s.u("dve", "dve", lambda: nc.vector.tensor_add(rank_sb, rank_sb, gt_scr),
            r=[bf("gts")], w=[bf("rank")], tag=f"gskip2{k}_{b}")
        s.u("dve", "dve", lambda: nc.vector.tensor_copy(ranki_sb, rank_sb),
            r=[bf("rank")], w=[bf("ranki")], tag=f"gcast{k}_{b}")
        sc_toks = []
        for j in range(96):
            t = s.u("pool", "qix", lambda j=j, k=k, b=b: nc.gpsimd.indirect_dma_start(
                out=idx_sp[(k, b)],
                out_offset=bass.IndirectOffsetOnAxis(ap=ranki_sb[:, j:j + 1], axis=0),
                in_=fidx_sb[:, j:j + 1], in_offset=None,
                bounds_check=bc_reg, oob_is_err=False),
                r=[bf("ranki"), bf("fidx")],
                deps=ind_dep(), tag=f"gsc{k}_{b}_{j}")
            sc_toks.append(t)
            ind_toks.append(t)
        s.u("sp", "qsp", lambda k=k, b=b: nc.sync.dma_start(
            out=idxf_sb.rearrange("p (g o) -> p g o", o=1),
            in_=idx_sp[(k, b)].rearrange("(p g) o -> p g o", g=32)),
            deps=sc_toks, w=[bf("idxf")], tag=f"gld{k}_{b}")
        s.u("dve", "dve", lambda k=k, b=b: nc.vector.tensor_copy(
            idxi_sb[(k, b)], idxf_sb), r=[bf("idxf")], w=[bf(f"idxi{k}_{b}")],
            tag=f"gldc{k}_{b}")

    def ffn_iter_b(k, b):
        xin_t = {1: xin1_sp, 2: xn_sp[1], 3: xin3_sp}[k]
        xinB = {1: bf("xin1_sp"), 2: bf(f"xn1_{b}"), 3: None}[k]
        nfc = 32
        ngr = nfc // G
        accB = bf("acc")
        if True:
            for fg in range(ngr):
                wmr = wm_rb[fg % 2]
                wmrB = bf(f"wm_rb{fg % 2}")
                wpr = wp_rb[fg % 2]
                wprB = bf(f"wp_rb{fg % 2}")
                if k == 1:
                    s.u("pool", "qgp", lambda fg=fg, wmr=wmr: nc.gpsimd.dma_start(
                        out=wmr.rearrange("p (kc fm) -> p kc fm", kc=8),
                        in_=wm[:, fg * G * 128:(fg + 1) * G * 128].rearrange(
                            "(kc p) fm -> p kc fm", p=128)),
                        w=[wmrB], tag=f"ldwm{k}_{b}_{fg}")
                    s.u("pool", "qgp", lambda fg=fg, wpr=wpr: nc.gpsimd.dma_start(
                        out=wpr.rearrange("p (f c) -> p f c", f=G),
                        in_=wp[fg * G * 128:(fg + 1) * G * 128, :].rearrange(
                            "(f p) c -> p f c", p=128)),
                        w=[wprB], tag=f"ldwp{k}_{b}_{fg}")
                else:
                    idxB = bf(f"idxi{k}_{b}")
                    for gch in range(G):
                        g = fg * G + gch
                        wg = wmg_f32[g % 4]
                        wgB = bf(f"wmg{g % 4}")
                        t = s.u("pool", "qgp", lambda g=g, wg=wg, k=k, b=b: nc.gpsimd.indirect_dma_start(
                            out=wg, out_offset=None, in_=wmR,
                            in_offset=bass.IndirectOffsetOnAxis(
                                ap=idxi_sb[(k, b)][:, g:g + 1], axis=0)),
                            r=[idxB], w=[wgB], deps=ind_dep(), tag=f"gwm{k}_{b}_{g}")
                        ind_toks.append(t)
                        for kc in range(NCC):
                            ptr = pb[6 + (kc % 2)][:, :128]
                            s.u("pe", "pe", lambda ptr=ptr, wg=wg, kc=kc: nc.tensor.transpose(
                                ptr, wg[:, kc * 128:(kc + 1) * 128], ident_sb),
                                r=[wgB, bf("ident")], w=[pbuf[6 + (kc % 2)]],
                                tag=f"gtr{k}_{b}_{g}_{kc}")
                            s.u("dve", "dve", lambda ptr=ptr, wmr=wmr, kc=kc, gch=gch: nc.vector.tensor_copy(
                                wmr[:, kc * G * 128 + gch * 128: kc * G * 128 + (gch + 1) * 128], ptr),
                                r=[pbuf[6 + (kc % 2)]], w=[wmrB],
                                tag=f"gtev{k}_{b}_{g}_{kc}")
                        wpg = wpg_f32[g % 4]
                        wpgB = bf(f"wpg{g % 4}")
                        t = s.u("pool", "qgp", lambda g=g, wpg=wpg, k=k, b=b: nc.gpsimd.indirect_dma_start(
                            out=wpg, out_offset=None, in_=wp,
                            in_offset=bass.IndirectOffsetOnAxis(
                                ap=idxi_sb[(k, b)][:, g:g + 1], axis=0)),
                            r=[idxB], w=[wpgB], deps=ind_dep(), tag=f"gwp{k}_{b}_{g}")
                        ind_toks.append(t)
                        s.u("act", "act", lambda wpg=wpg, wpr=wpr, gch=gch: nc.scalar.activation(
                            wpr[:, gch * 1024:(gch + 1) * 1024], wpg, AF.Copy),
                            r=[wpgB], w=[wprB], tag=f"gwpc{k}_{b}_{g}")
                for tt in range(L // TT):
                    slot = xi_ctr[0] % 3
                    xi_ctr[0] += 1
                    xi = xi_rb[slot]
                    xiB = bf(f"xi{slot}")
                    xinBt = bf(f"xin3_{b}_{tt}") if k == 3 else xinB
                    s.u("sp", "qsp", lambda b=b, tt=tt, xi=xi: nc.sync.dma_start(
                        out=xi.rearrange("p (cc t) -> p cc t", cc=NCC),
                        in_=xin_t[b, :, :, tt * TT:(tt + 1) * TT].rearrange("cc p t -> p cc t")),
                        r=[xinBt], w=[xiB], tag=f"ldxi{k}_{b}_{fg}_{tt}")
                    hB = bf("hbuf")
                    for fl in range(G):
                        f = fg * G + fl
                        bank = fl % 2
                        phh = pb[bank][:, :TT]
                        for kc in range(NCC):
                            s.u("pe", "pe", lambda phh=phh, wmr=wmr, kc=kc, fl=fl, xi=xi: nc.tensor.matmul(
                                phh, wmr[:, kc * G * 128 + fl * 128: kc * G * 128 + (fl + 1) * 128],
                                xi[:, kc * TT:(kc + 1) * TT], start=(kc == 0), stop=(kc == NCC - 1)),
                                r=[wmrB, xiB], w=[pbuf[bank]], tag=f"wm{k}_{b}_{fg}_{tt}_{fl}")
                        s.u("act", "act", lambda phh=phh, fl=fl: nc.scalar.activation(
                            hbuf[:, fl * TT:(fl + 1) * TT], phh, AF.Relu),
                            r=[pbuf[bank]], w=[hB], tag=f"hrelu{k}_{b}_{fg}_{tt}_{fl}")
                    if DEBUG_FFN and k == 1 and b == 0 and fg == 0 and tt == 0:
                        s.u("sp", "qsp", lambda: nc.sync.dma_start(
                            out=T["dbg_h"], in_=hbuf), r=[bf("hbuf")], tag="dumph")
                        s.u("sp", "qsp", lambda xi=xi: nc.sync.dma_start(
                            out=T["dbg_xi"], in_=xi), r=[bf(f"xi{(xi_ctr[0]-1) % 3}")], tag="dumpxi")
                    for cc in range(NCC):
                        bank = 2 + (cc % 2)
                        pxn = pb[bank][:, :TT]
                        for fl in range(G):
                            s.u("pe", "pe", lambda pxn=pxn, wpr=wpr, fl=fl, cc=cc: nc.tensor.matmul(
                                pxn, wpr[:, fl * 1024 + cc * 128: fl * 1024 + (cc + 1) * 128],
                                hbuf[:, fl * TT:(fl + 1) * TT], start=(fl == 0), stop=(fl == G - 1)),
                                r=[wprB, hB], w=[pbuf[bank]], tag=f"wp{k}_{b}_{fg}_{tt}_{cc}")
                        accv = acc[:, cc * L + tt * TT: cc * L + (tt + 1) * TT]
                        if fg == 0:
                            s.u("dve", "dve", lambda pxn=pxn, accv=accv: nc.vector.tensor_copy(
                                accv, pxn), r=[pbuf[bank]], w=[accB],
                                tag=f"acc0{k}_{b}_{fg}_{tt}_{cc}")
                        else:
                            s.u("dve", "dve", lambda pxn=pxn, accv=accv: nc.vector.tensor_add(
                                accv, accv, pxn), r=[pbuf[bank]], w=[accB],
                                tag=f"accp{k}_{b}_{fg}_{tt}_{cc}")
            if DEBUG_FFN and k == 1 and b == 0:
                s.u("sp", "qsp", lambda: nc.sync.dma_start(
                    out=T["dbg_acc"], in_=acc), r=[bf("acc")], tag="dumpacc")
        if True:
            for cc in range(NCC):
                s.u("dve", "dve", lambda cc=cc, b=b: nc.vector.tensor_scalar(
                    hbuf[:, :L], acc[:, cc * L:(cc + 1) * L], 1.0, 0.0, OP.mult, OP.add,
                    accum_out=pooled_sb[:, cc * 2 + b: cc * 2 + b + 1]),
                    r=[accB], w=[bf("hbuf"), bf("pooled")], tag=f"pool{k}_{b}_{cc}")
            s.u("sp", "qsp", lambda b=b: nc.sync.dma_start(
                out=xn_sp[k][b].rearrange("cc p t -> p cc t"), in_=acc.rearrange("p (cc t) -> p cc t", cc=NCC)),
                r=[accB], w=[bf(f"xn{k}_{b}")], tag=f"stxn{k}_{b}")

    def gating_pre(k, b):
        for m in range(2):
            py1 = pb[6 + m][:, :1]
            for kc in range(NCC):
                s.u("pe", "pe", lambda py1=py1, kc=kc, m=m, b=b: nc.tensor.matmul(
                    py1, wc1_sb[:, kc * 256 + m * 128: kc * 256 + (m + 1) * 128],
                    pooled_sb[:, kc * 2 + b: kc * 2 + b + 1], start=(kc == 0), stop=(kc == NCC - 1)),
                    r=[bf("wc1"), bf("pooled")], w=[pbuf[6 + m]], tag=f"y1mm{k}_{b}_{m}_{kc}")
            s.u("act", "act", lambda py1=py1, m=m, b=b: nc.scalar.activation(
                y1_bf[:, m * 2 + b: m * 2 + b + 1], py1, AF.Relu), r=[pbuf[6 + m]],
                w=[bf("y1")], tag=f"y1relu{k}_{b}_{m}")
        for piece in range(12):
            wcr = wc2_rb[piece % 2]
            wcrB = bf(f"wc2_rb{piece % 2}")
            if True:
                s.u("pool", "qgp", lambda piece=piece, wcr=wcr: nc.gpsimd.dma_start(
                    out=wcr.rearrange("p (m f) -> p m f", m=2),
                    in_=wc2[:, piece * 1024:(piece + 1) * 1024].rearrange(
                        "(m p) f -> p m f", p=128)), w=[wcrB], tag=f"ldwc2_{k}_{piece}")
            for fl in range(8):
                f = piece * 8 + fl
                bank = 6 + (f % 2)
                pck = pb[bank][:, :1]
                for m in range(2):
                    s.u("pe", "pe", lambda pck=pck, fl=fl, m=m, wcr=wcr, b=b: nc.tensor.matmul(
                        pck, wcr[:, m * 1024 + fl * 128: m * 1024 + (fl + 1) * 128],
                        y1_bf[:, m * 2 + b: m * 2 + b + 1], start=(m == 0), stop=(m == 1)),
                        r=[wcrB, bf("y1")], w=[pbuf[bank]], tag=f"ckmm{k}_{b}_{f}_{m}")
                s.u("dve", "dve", lambda pck=pck, f=f, b=b: nc.vector.tensor_copy(
                    ck_sb[:, b * 96 + f: b * 96 + f + 1], pck),
                    r=[pbuf[bank]], w=[bf("ck")], tag=f"ckevac{k}_{b}_{f}")
    def gating_row(k, b):
        if True:
            ckv = ck_sb[:, b * 96:(b + 1) * 96]
            s.u("dve", "dve", lambda ckv=ckv, b=b: nc.vector.reduce_max(
                bis["cnt"][b], ckv, AX.X, apply_absolute_value=True), r=[bf("ck")],
                w=[bf(f"bcnt{b}")], tag=f"bamax{k}_{b}")
            s.u("dve", "dve", lambda: nc.vector.memset(tpad_sb, NEG_BIG),
                w=[bf("tpad")], tag=f"bpad{k}_{b}")
            s.u("dve", "dve", lambda b=b: nc.vector.tensor_copy(
                tpad_sb[:, 0:1], bis["cnt"][b]), r=[bf(f"bcnt{b}")], w=[bf("tpad")],
                tag=f"bpadc{k}_{b}")
            ptr = pb[4][:, :128]
            s.u("pe", "pe", lambda ptr=ptr: nc.tensor.transpose(ptr, tpad_sb, ident_sb),
                r=[bf("tpad"), bf("ident")], w=[pbuf[4]], tag=f"btr{k}_{b}")
            s.u("dve", "dve", lambda ptr=ptr: nc.vector.reduce_max(
                colrow[:, 0:1], ptr[0:1, :], AX.X), r=[pbuf[4]], w=[bf("crow")],
                tag=f"bamax2{k}_{b}")
            pbc = pb[4][:, :1]
            s.u("pe", "pe", lambda pbc=pbc: nc.tensor.matmul(
                pbc, ones1_sb, colrow[:1, 0:1], start=True, stop=True),
                r=[bf("crow"), bf("ones1")], w=[pbuf[4]], tag=f"bhl{k}_{b}")
            s.u("dve", "dve", lambda b=b, pbc=pbc: nc.vector.tensor_copy(
                bis["hi"][b], pbc), r=[pbuf[4]], w=[bf(f"bhi{b}")], tag=f"bhi0{k}_{b}")
            s.u("dve", "dve", lambda b=b, pbc=pbc: nc.vector.tensor_scalar(
                bis["lo"][b], pbc, -1.0, -1.0, OP.mult, OP.add), r=[pbuf[4]],
                w=[bf(f"blo{b}")], tag=f"blo0{k}_{b}")
            s.u("dve", "dve", lambda b=b: nc.vector.tensor_add(
                bis["th"][b], bis["lo"][b], bis["hi"][b]), r=[bf(f"blo{b}"), bf(f"bhi{b}")],
                w=[bf(f"bth{b}")], tag=f"bth0{k}_{b}")
            s.u("dve", "dve", lambda b=b: nc.vector.tensor_scalar_mul(
                bis["th"][b], bis["th"][b], 0.5), w=[bf(f"bth{b}")], tag=f"bth00{k}_{b}")
            for it in range(BIS_ITERS):
                s.u("dve", "dve", lambda ckv=ckv, b=b: nc.vector.tensor_scalar(
                    gt_scr, ckv, bis["th"][b], 0.0, OP.is_gt, OP.add,
                    accum_out=bis["cnt"][b]), r=[bf("ck"), bf(f"bth{b}")],
                    w=[bf("gts"), bf(f"bcnt{b}")], tag=f"bcnt{k}_{b}_{it}")
                pc = pb[4][:1, :1]
                s.u("pe", "pe", lambda pc=pc, b=b: nc.tensor.matmul(
                    pc, ones_sb, bis["cnt"][b], start=True, stop=True),
                    r=[bf("ones"), bf(f"bcnt{b}")], w=[pbuf[4]], tag=f"bsum{k}_{b}_{it}")
                s.u("act", "act", lambda pc=pc: nc.scalar.activation(
                    scal_sb[:, 0:1], pc, AF.Copy), r=[pbuf[4]], w=[bf("scal")],
                    tag=f"bsc{k}_{b}_{it}")
                pb4 = pb[4][:, :1]
                s.u("pe", "pe", lambda pb4=pb4: nc.tensor.matmul(
                    pb4, ones1_sb, scal_sb[:1, 0:1], start=True, stop=True),
                    r=[bf("scal"), bf("ones1")], w=[pbuf[4]], tag=f"bbc{k}_{b}_{it}")
                s.u("dve", "dve", lambda pb4=pb4, b=b: nc.vector.tensor_scalar(
                    bis["sel"][b], pb4, float(ITER), None, OP.is_ge),
                    r=[pbuf[4]], w=[bf(f"bsel{b}")], tag=f"bsel{k}_{b}_{it}")
                s.u("dve", "dve", lambda b=b: nc.vector.select(
                    bis["lo"][b], bis["sel"][b], bis["th"][b], bis["lo"][b]),
                    r=[bf(f"bsel{b}"), bf(f"bth{b}")], w=[bf(f"blo{b}")],
                    tag=f"bsl{k}_{b}_{it}")
                s.u("dve", "dve", lambda b=b: nc.vector.select(
                    bis["hi"][b], bis["sel"][b], bis["hi"][b], bis["th"][b]),
                    r=[bf(f"bsel{b}"), bf(f"bth{b}")], w=[bf(f"bhi{b}")],
                    tag=f"bsh{k}_{b}_{it}")
                s.u("dve", "dve", lambda b=b: nc.vector.tensor_add(
                    bis["th"][b], bis["lo"][b], bis["hi"][b]),
                    r=[bf(f"blo{b}"), bf(f"bhi{b}")], w=[bf(f"bth{b}")],
                    tag=f"bt{k}_{b}_{it}")
                s.u("dve", "dve", lambda b=b: nc.vector.tensor_scalar_mul(
                    bis["th"][b], bis["th"][b], 0.5), w=[bf(f"bth{b}")],
                    tag=f"bt2{k}_{b}_{it}")
            s.u("dve", "dve", lambda ckv=ckv, b=b: nc.vector.tensor_scalar(
                gt_scr, ckv, bis["hi"][b], 0.0, OP.is_gt, OP.add, accum_out=bis["cnt"][b]),
                r=[bf("ck"), bf(f"bhi{b}")], w=[bf("gts"), bf(f"bcnt{b}")],
                tag=f"mgt{k}_{b}")
            s.u("dve", "dve", lambda ckv=ckv, b=b: nc.vector.tensor_scalar(
                gt_scr2, ckv, bis["lo"][b], None, OP.is_gt), r=[bf("ck"), bf(f"blo{b}")],
                w=[bf("gts2")], tag=f"mge{k}_{b}")
            s.u("dve", "dve", lambda: nc.vector.tensor_sub(gt_scr2, gt_scr2, gt_scr),
                r=[bf("gts")], w=[bf("gts2")], tag=f"minT{k}_{b}")
            pc = pb[4][:1, :1]
            s.u("pe", "pe", lambda pc=pc, b=b: nc.tensor.matmul(
                pc, ones_sb, bis["cnt"][b], start=True, stop=True),
                r=[bf("ones"), bf(f"bcnt{b}")], w=[pbuf[4]], tag=f"msum{k}_{b}")
            s.u("act", "act", lambda pc=pc: nc.scalar.activation(
                scal_sb[:, 1:2], pc, AF.Copy), r=[pbuf[4]], w=[bf("scal")],
                tag=f"msc{k}_{b}")
            pb4 = pb[4][:, :1]
            s.u("pe", "pe", lambda pb4=pb4: nc.tensor.matmul(
                pb4, ones1_sb, scal_sb[:1, 1:2], start=True, stop=True),
                r=[bf("scal"), bf("ones1")], w=[pbuf[4]], tag=f"mbc{k}_{b}")
            s.u("dve", "dve", lambda pb4=pb4, b=b: nc.vector.tensor_scalar(
                bis["r"][b], pb4, -1.0, float(ITER), OP.mult, OP.add),
                r=[pbuf[4]], w=[bf(f"br{b}")], tag=f"mr{k}_{b}")
            p96 = pb[4][:1, :96]
            s.u("pe", "pe", lambda p96=p96: nc.tensor.matmul(
                p96, ones_sb, gt_scr2, start=True, stop=True),
                r=[bf("ones"), bf("gts2")], w=[pbuf[4]], tag=f"ccol{k}_{b}")
            s.u("dve", "dve", lambda p96=p96: nc.vector.tensor_tensor_scan(
                colrow[:, 96:192], ones1_sb[:1, :96], p96, 0.0, OP.mult, OP.add),
                r=[pbuf[4], bf("ones1")], w=[bf("crow")], tag=f"ccum{k}_{b}")
            s.u("dve", "dve", lambda p96=p96: nc.vector.tensor_sub(
                colrow[:, 96:192], colrow[:, 96:192], p96), r=[pbuf[4]],
                w=[bf("crow")], tag=f"cex{k}_{b}")
            pc96 = pb[4][:, :96]
            s.u("pe", "pe", lambda pc96=pc96: nc.tensor.matmul(
                pc96, ones1_sb, colrow[:1, 96:192], start=True, stop=True),
                r=[bf("crow"), bf("ones1")], w=[pbuf[4]], tag=f"cbc{k}_{b}")
            s.u("dve", "dve", lambda pc96=pc96: nc.vector.tensor_copy(cbc_sb, pc96),
                r=[pbuf[4]], w=[bf("cbc")], tag=f"cbcc{k}_{b}")
            pp = pb[5][:, :96]
            s.u("pe", "pe", lambda pp=pp: nc.tensor.matmul(
                pp, ltri_sb, gt_scr2, start=True, stop=True),
                r=[bf("ltri"), bf("gts2")], w=[pbuf[5]], tag=f"pfx{k}_{b}")
            s.u("dve", "dve", lambda pp=pp: nc.vector.tensor_add(rank_sb, cbc_sb, pp),
                r=[bf("cbc"), pbuf[5]], w=[bf("rank")], tag=f"rank{k}_{b}")
            s.u("dve", "dve", lambda b=b: nc.vector.tensor_scalar(
                rank_sb, rank_sb, bis["r"][b], None, OP.is_lt), r=[bf(f"br{b}")],
                w=[bf("rank")], tag=f"fill{k}_{b}")
            s.u("dve", "dve", lambda: nc.vector.tensor_mul(rank_sb, rank_sb, gt_scr2),
                r=[bf("gts2")], w=[bf("rank")], tag=f"fill2{k}_{b}")
            mk = mask_sb[(k, b)]
            mkB = bf(f"mask{k}_{b}")
            s.u("dve", "dve", lambda mk=mk: nc.vector.tensor_add(mk, gt_scr, rank_sb),
                r=[bf("gts"), bf("rank")], w=[mkB], tag=f"mk{k}_{b}")
            hist = bhist_sb if k == 1 else hist2_sb[b]
            histB = bf("bhist") if k == 1 else bf(f"hist2_{b}")
            s.u("dve", "dve", lambda mk=mk, hist=hist: nc.vector.tensor_mul(
                gt_scr, mk, hist), r=[mkB, histB], w=[bf("gts")], tag=f"ovl{k}_{b}")
            s.u("dve", "dve", lambda b=b: nc.vector.tensor_scalar(
                gt_scr, gt_scr, 1.0, 0.0, OP.mult, OP.add, accum_out=bis["cnt"][b]),
                w=[bf("gts"), bf(f"bcnt{b}")], tag=f"ovls{k}_{b}")
            pc = pb[4][:1, :1]
            s.u("pe", "pe", lambda pc=pc, b=b: nc.tensor.matmul(
                pc, ones_sb, bis["cnt"][b], start=True, stop=True),
                r=[bf("ones"), bf(f"bcnt{b}")], w=[pbuf[4]], tag=f"ovlm{k}_{b}")
            s.u("act", "act", lambda pc=pc, b=b: nc.scalar.activation(
                scal_sb[:, 4 + b:5 + b], pc, AF.Copy), r=[pbuf[4]], w=[bf("scal")],
                tag=f"ovle{k}_{b}")
            if k == 1:
                s.u("dve", "dve", lambda mk=mk, b=b: nc.vector.tensor_max(
                    hist2_sb[b], bhist_sb, mk), r=[mkB, bf("bhist")],
                    w=[bf(f"hist2_{b}")], tag=f"hist2_{b}")
    def gating_cc(k):
        s.u("dve", "dve", lambda: nc.vector.tensor_add(
            scal_sb[:, 6:7], scal_sb[:, 4:5], scal_sb[:, 5:6]), w=[bf("scal")],
            tag=f"ovladd{k}")
        s.u("dve", "dve", lambda: nc.vector.memset(scal_sb[:, 8:16], 0.0),
            w=[bf("scal")], tag=f"ccz{k}")
        s.u("dve", "dve", lambda: nc.vector.tensor_copy(
            scal_sb[:, 8:9], scal_sb[:, 6:7]), w=[bf("scal")], tag=f"ccc{k}")
        s.u("pool", "qgp", lambda: nc.gpsimd.dma_start(
            out=cc_in[k], in_=scal_sb[:1, 8:16]), r=[bf("scal")], w=[bf(f"cci{k}")],
            tag=f"ccw{k}")
        s.u("pool", "qcc", lambda: nc.gpsimd.collective_compute(
            "AllReduce", OP.add, replica_groups=[[i for i in range(8)]],
            ins=[cc_in[k].opt()], outs=[cc_out[k].opt()]), r=[bf(f"cci{k}")], w=[bf(f"cco{k}")],
            tag=f"ccar{k}")
        s.u("pool", "qgp", lambda: nc.gpsimd.dma_start(
            out=scal_sb[:1, 8:16], in_=cc_out[k]), r=[bf(f"cco{k}")], w=[bf("scal")],
            tag=f"ccr{k}")
        s.u("dve", "dve", lambda: nc.vector.tensor_scalar(
            scal_sb[:, 1 + k: 2 + k], scal_sb[:, 8:9], OVL_TH, None, OP.is_gt),
            w=[bf("scal")], tag=f"dk{k}")

    def bcast_d(nm, col):
        pb4 = pb[4][:, :1]
        s.u("pe", "pe", lambda pb4=pb4, col=col: nc.tensor.matmul(
            pb4, ones1_sb, scal_sb[:1, col:col + 1], start=True, stop=True),
            r=[bf("scal"), bf("ones1")], w=[pbuf[4]], tag=f"dbc{nm}")
        s.u("dve", "dve", lambda pb4=pb4, nm=nm: nc.vector.tensor_copy(
            d_bc[nm], pb4), r=[pbuf[4]], w=[bf(f"dbc{nm}")], tag=f"dbcc{nm}")

    def post_d3():
        s.u("dve", "dve", lambda: nc.vector.tensor_max(
            scal_sb[:, 3:4], scal_sb[:, 2:3], scal_sb[:, 3:4]), w=[bf("scal")],
            tag="d3max")
        bcast_d("d3", 3)

    def post_mask3_b(b):
        m1, m2, m3 = mask_sb[(1, b)], mask_sb[(2, b)], mask_sb[(3, b)]
        s.u("dve", "dve", lambda m1=m1, m2=m2: nc.vector.tensor_sub(
            gt_scr, m1, m2), r=[bf(f"mask1_{b}"), bf(f"mask2_{b}")], w=[bf("gts")],
            tag=f"m3a{b}")
        s.u("dve", "dve", lambda: nc.vector.tensor_scalar(
            gt_scr, gt_scr, d_bc["d2"], None, OP.mult), r=[bf("dbcd2")],
            w=[bf("gts")], tag=f"m3b{b}")
        s.u("dve", "dve", lambda m2=m2, m3=m3, b=b: nc.vector.tensor_add(
            m3, m2, gt_scr), r=[bf("gts"), bf(f"mask2_{b}")], w=[bf(f"mask3_{b}")],
            tag=f"m3c{b}")

    def make_xin3_b(b):
        for tt in range(L // TT):
            x1t, x2t, x3t = xi_rb[0], xi_rb[1], xi_rb[2]
            s.u("sp", "qsp", lambda x1t=x1t, b=b, tt=tt: nc.sync.dma_start(
                out=x1t.rearrange("p (cc t) -> p cc t", cc=NCC),
                in_=xn_sp[1][b, :, :, tt * TT:(tt + 1) * TT].rearrange("cc p t -> p cc t")), r=[bf(f"xn1_{b}")],
                w=[bf("xi0")], tag=f"x3l1{b}_{tt}")
            s.u("sp", "qsp", lambda x2t=x2t, b=b, tt=tt: nc.sync.dma_start(
                out=x2t.rearrange("p (cc t) -> p cc t", cc=NCC),
                in_=xn_sp[2][b, :, :, tt * TT:(tt + 1) * TT].rearrange("cc p t -> p cc t")), r=[bf(f"xn2_{b}")],
                w=[bf("xi1")], tag=f"x3l2{b}_{tt}")
            for cc in range(NCC):
                s.u("dve", "dve", lambda x1t=x1t, x2t=x2t, cc=cc: nc.vector.tensor_sub(
                    fin_sc, x1t[:, cc * TT:(cc + 1) * TT], x2t[:, cc * TT:(cc + 1) * TT]),
                    r=[bf("xi0"), bf("xi1")], w=[bf("fin")], tag=f"x3s{b}_{tt}_{cc}")
                s.u("dve", "dve", lambda x2t=x2t, x3t=x3t, cc=cc: nc.vector.scalar_tensor_tensor(
                    x3t[:, cc * TT:(cc + 1) * TT], fin_sc, d_bc["d2"],
                    x2t[:, cc * TT:(cc + 1) * TT], OP.mult, OP.add),
                    r=[bf("fin"), bf("xi1"), bf("dbcd2")], w=[bf("xi2")],
                    tag=f"x3b{b}_{tt}_{cc}")
            s.u("act", "qact", lambda x3t=x3t, b=b, tt=tt: nc.scalar.dma_start(
                out=xin3_sp[b, :, :, tt * TT:(tt + 1) * TT].rearrange("cc p t -> p cc t"),
                in_=x3t.rearrange("p (cc t) -> p cc t", cc=NCC)), r=[bf("xi2")],
                w=[bf(f"xin3_{b}_{tt}")], tag=f"x3w{b}_{tt}")

    if "ffn" in stages:
        ffn_iter_b(1, 0)
        if 2 in DEBUG_ITERS:
            gating_pre(1, 0)
            gating_row(1, 0)
            build_indices(2, 0)
            ffn_iter_b(1, 1)
            gating_pre(1, 1)
            gating_row(1, 1)
            ffn_iter_b(2, 0)
            build_indices(2, 1)
            gating_cc(1)
            bcast_d("d2", 2)
            ffn_iter_b(2, 1)
            gating_pre(2, 0)
            gating_row(2, 0)
            post_mask3_b(0)
            build_indices(3, 0)
            gating_pre(2, 1)
            gating_row(2, 1)
            post_mask3_b(1)
            build_indices(3, 1)
            make_xin3_b(0)
            make_xin3_b(1)
            ffn_iter_b(3, 0)
            gating_cc(2)
            post_d3()
            ffn_iter_b(3, 1)
        else:
            ffn_iter_b(1, 1)
    ph2.close()
    s.fence()

    ph3 = contextlib.ExitStack()
    f_a2 = [sb(f"f_a{i}", [128, NCC * TT], BF16, ph3) for i in range(2)]
    f_b2 = [sb(f"f_b{i}", [128, NCC * TT], BF16, ph3) for i in range(2)]
    f_x1 = [sb(f"f_x1{i}", [128, NCC * TT], F32, ph3) for i in range(2)]
    f_fin = [sb(f"f_fin{i}", [128, NCC * TT], F32, ph3) for i in range(2)]
    f_o = [sb(f"f_o{i}", [128, NCC * TT], F32, ph3) for i in range(2)]

    def final():
        out_toks = []
        for b in range(B):
            for tt in range(L // TT):
                sl = tt % 2
                ff = f_fin[sl]
                ffB = bf(f"f_fin{sl}")
                fx = f_x1[sl]
                fxB = bf(f"f_x1{sl}")
                fo = f_o[sl]
                foB = bf(f"f_o{sl}")
                f_a = f_a2[sl]
                faB = bf(f"f_a{sl}")
                f_b = f_b2[sl]
                fbB = bf(f"f_b{sl}")
                s.u("sp", "qsp", lambda b=b, tt=tt, f_a=f_a: nc.sync.dma_start(
                    out=f_a.rearrange("p (cc t) -> p cc t", cc=NCC),
                    in_=xin3_sp[b, :, :, tt * TT:(tt + 1) * TT].rearrange("cc p t -> p cc t")),
                    r=[bf(f"xin3_{b}_{tt}")], w=[faB], tag=f"fl1{b}_{tt}")
                s.u("act", "qact", lambda b=b, tt=tt, f_b=f_b: nc.scalar.dma_start(
                    out=f_b.rearrange("p (cc t) -> p cc t", cc=NCC),
                    in_=xn_sp[3][b, :, :, tt * TT:(tt + 1) * TT].rearrange("cc p t -> p cc t")),
                    r=[bf(f"xn3_{b}")], w=[fbB], tag=f"fl2{b}_{tt}")
                s.u("sp", "qsp", lambda b=b, tt=tt, fx=fx: nc.sync.dma_start(
                    out=fx.rearrange("p (cc t) -> p cc t", cc=NCC),
                    in_=x1_sp[b, :, :, tt * TT:(tt + 1) * TT].rearrange("cc p t -> p cc t")),
                    r=[bf("x1_sp")], w=[fxB], tag=f"fl3{b}_{tt}")
                for cc in range(NCC):
                    s.u("dve", "dve", lambda ff=ff, cc=cc, f_a=f_a, f_b=f_b: nc.vector.tensor_sub(
                        ff[:, cc * TT:(cc + 1) * TT], f_a[:, cc * TT:(cc + 1) * TT],
                        f_b[:, cc * TT:(cc + 1) * TT]),
                        r=[faB, fbB], w=[ffB], tag=f"fs{b}_{tt}_{cc}")
                    s.u("dve", "dve", lambda ff=ff, cc=cc, f_b=f_b: nc.vector.scalar_tensor_tensor(
                        ff[:, cc * TT:(cc + 1) * TT], ff[:, cc * TT:(cc + 1) * TT],
                        d_bc["d3"], f_b[:, cc * TT:(cc + 1) * TT], OP.mult, OP.add),
                        r=[fbB, bf("dbcd3")], w=[ffB], tag=f"fb{b}_{tt}_{cc}")
                srcs = [(ff[:, cc * TT:(cc + 1) * TT], ffB) for cc in range(NCC)]
                mbc, rbc, mB, rB = ln(srcs, TT, f"ln4_{b}_{tt}")
                for cc in range(NCC):
                    ap = srcs[cc][0]
                    s.u("dve", "dve", lambda ap=ap, cc=cc, fo=fo: nc.vector.tensor_sub(
                        fo[:, cc * TT:(cc + 1) * TT], ap, mbc), r=[ffB, mB], w=[foB],
                        tag=f"f4a{b}_{tt}_{cc}")
                    s.u("dve", "dve", lambda cc=cc, fo=fo: nc.vector.tensor_mul(
                        fo[:, cc * TT:(cc + 1) * TT], fo[:, cc * TT:(cc + 1) * TT], rbc),
                        r=[rB], w=[foB], tag=f"f4b{b}_{tt}_{cc}")
                    s.u("dve", "dve", lambda cc=cc, fo=fo, fx=fx: nc.vector.tensor_add(
                        fo[:, cc * TT:(cc + 1) * TT], fo[:, cc * TT:(cc + 1) * TT],
                        fx[:, cc * TT:(cc + 1) * TT]), r=[fxB], w=[foB],
                        tag=f"f4c{b}_{tt}_{cc}")
                t = s.u("sp", "qsp", lambda b=b, tt=tt, fo=fo: nc.sync.dma_start(
                    out=out[b, :, :, tt * TT:(tt + 1) * TT].rearrange("cc p t -> p cc t"),
                    in_=fo.rearrange("p (cc t) -> p cc t", cc=NCC)), r=[foB],
                    tag=f"fw{b}_{tt}")
                out_toks.append(t)
        s.final_deps.extend(out_toks)

    if "final" in stages:
        final()
    ph3.close()

    s.emit_all()
    return T


def _consts():
    f32 = np.float32
    ltri = np.zeros((128, 128), f32)
    for p in range(128):
        ltri[:p, p] = 1.0
    bhist = np.zeros((128, 96), f32)
    bhist[:, :32] = 1.0
    oneh = np.zeros((16, 16 * 128), f32)
    for n in range(16):
        oneh[n, n * 128:(n + 1) * 128] = 1.0
    return ltri, bhist, oneh


def _prep_core(ci, inp):
    f32 = np.float32
    x = inp["L0_attn_x"][2 * ci: 2 * ci + 2]
    xr = np.ascontiguousarray(x.transpose(0, 2, 1)).reshape(B, NCC, 128, L)
    cb = np.zeros((128, NCB), f32)
    cb[:, C_INB:C_INB + 16] = inp["L0_attn_in_b"].reshape(16, 128).T
    cb[:, C_DTB:C_DTB + 16] = inp["L0_attn_dt_b"].reshape(16, 128).T
    cb[:, C_CONVW:C_CONVW + 48] = np.asarray(inp["L0_attn_conv_w"]).reshape(16, 128, 3).transpose(1, 0, 2).reshape(128, 48)
    cb[:, C_CONVB:C_CONVB + 16] = inp["L0_attn_conv_b"].reshape(16, 128).T
    cb[:, C_D:C_D + 16] = inp["L0_attn_D_param"].reshape(16, 128).T
    cb[:, C_OUTB:C_OUTB + 8] = inp["L0_attn_out_b"].reshape(8, 128).T
    cb[:64, C_XPB] = inp["L0_attn_x_proj_b"][:64]
    cb[:16, C_XPBM] = inp["L0_attn_x_proj_b"][64:]
    for i, v in enumerate(K_VALS):
        cb[:, C_K + i] = v
    cb[:, C_ALOG:] = np.asarray(inp["L0_attn_A_log"]).reshape(16, 128, 16).transpose(1, 0, 2).reshape(128, 256)
    ltri, bhist, oneh = _consts()
    return dict(
        x=np.ascontiguousarray(xr, f32),
        wi=np.ascontiguousarray(np.asarray(inp["L0_attn_in_w"]).T, f32),
        wo=np.ascontiguousarray(np.asarray(inp["L0_attn_out_w"]).T, f32),
        wxp=np.ascontiguousarray(np.asarray(inp["L0_attn_x_proj_w"]).T, f32),
        wdt=np.ascontiguousarray(np.asarray(inp["L0_attn_dt_w"]).T, f32),
        wm=np.ascontiguousarray(np.asarray(inp["L0_ffn_Wm"]).T, f32),
        wmR=np.ascontiguousarray(np.asarray(inp["L0_ffn_Wm"]), f32),
        fidx_c=(np.arange(96, dtype=f32)[None, :] * 128 + np.arange(128, dtype=f32)[:, None]),
        wp=np.ascontiguousarray(np.asarray(inp["L0_ffn_Wp"]).T, f32),
        wc1=np.ascontiguousarray(np.asarray(inp["L0_ffn_Wc1"]).T, f32),
        wc2=np.ascontiguousarray(np.asarray(inp["L0_ffn_Wc2"]).T, f32),
        cb=cb,
        ones_c=np.ones((128, 128), f32),
        onescol_c=np.ones((128, 1), f32),
        onescolb_c=np.ones((128, 1), ml_dtypes.bfloat16),
        ident_c=np.eye(128, dtype=f32),
        ltri_c=ltri,
        bhist_c=bhist,
        oneh_c=oneh,
    )


def kernel(**inputs):
    inputs = {k: np.asarray(v) for k, v in inputs.items()}
    nc = bass.Bass("TRN2", target_bir_lowering=False, debug=False)
    build(nc)
    in_maps = [_prep_core(ci, inputs) for ci in range(8)]
    res = run_bass_kernel_spmd(nc, in_maps, core_ids=list(range(8)))
    outs = []
    for ci in range(8):
        o = np.asarray(res.results[ci]["out"])
        outs.append(o.reshape(B, D, L).transpose(0, 2, 1))
    return np.ascontiguousarray(np.concatenate(outs, axis=0), np.float32)
